# version 23
# baseline (speedup 1.0000x reference)
import contextlib
import numpy as np
import ml_dtypes

import concourse.bass as bass
import concourse.mybir as mybir
from concourse.bass_utils import run_bass_kernel_spmd

F32 = mybir.dt.float32
BF16 = mybir.dt.bfloat16
AF = mybir.ActivationFunctionType
ALU = mybir.AluOpType
AX = mybir.AxisListType.X

NT = 16
NI = 16
NH = 16
HD = 128
H = 2048
NCORES = 8

MASKVAL = float(np.float32(-1e30 * np.sqrt(128.0)))


def part_col(part, hh):
    return {"q": 0, "k": 2048, "v": 4096}[part] + hh * 128


class Prog:
    ENGINES = ("sync", "gpsimd", "scalar", "vector", "tensor")

    def __init__(self, nc, es):
        self.nc = nc
        self.es = es
        self.streams = {e: [] for e in self.ENGINES}
        self.sems = {}
        self.counts = {}
        self.writer = {}
        self.readers = {}
        self.waited = {e: {} for e in self.ENGINES}
        self.fence_mark = None
        self.final = []
        self.phase = 0
        self.max_phase = 99

    def _sem(self, name):
        if name not in self.sems:
            self.sems[name] = self.es.enter_context(self.nc.semaphore(name))
            self.counts[name] = 0
        return self.sems[name]

    def op(self, eng, build, reads=(), writes=(), dma=None):
        if self.phase > self.max_phase:
            return
        waits = []
        w = self.waited[eng]

        def need(semname, val):
            if semname == "c_tensor" and eng == "tensor":
                return
            if w.get(semname, 0) < val:
                waits.append((semname, val))
                w[semname] = val

        if self.fence_mark is not None:
            need(*self.fence_mark)
        for k in reads:
            wr = self.writer.get(k)
            if wr is not None:
                need(*wr)
        for k in writes:
            wr = self.writer.get(k)
            if wr is not None:
                need(*wr)
            for sv in self.readers.get(k, ()):
                need(*sv)
        if dma is not None:
            semname, step = "d_" + dma, 16
        else:
            semname, step = "c_" + eng, 1
        self._sem(semname)
        self.counts[semname] += step
        val = self.counts[semname]
        self.streams[eng].append((build, waits, semname, step))
        for k in writes:
            self.writer[k] = (semname, val)
            self.readers[k] = []
        for k in reads:
            self.readers.setdefault(k, []).append((semname, val))

    def fence(self):
        marks = {}
        if self.fence_mark is not None:
            marks[self.fence_mark[0]] = self.fence_mark[1]
        for wr in self.writer.values():
            marks[wr[0]] = max(marks.get(wr[0], 0), wr[1])
        for rl in self.readers.values():
            for (s, v) in rl:
                marks[s] = max(marks.get(s, 0), v)
        waits = [(s, v) for s, v in marks.items()]
        self._sem("c_vector")
        self.counts["c_vector"] += 1
        val = self.counts["c_vector"]
        self.streams["vector"].append((lambda e: e.engine_nop(), waits, "c_vector", 1))
        self.fence_mark = ("c_vector", val)
        self.writer = {}
        self.readers = {}
        self.waited = {e: {} for e in self.ENGINES}

    def emit(self, blk):
        waited_vals = {}
        for st in self.streams.values():
            for _, waits, _, _ in st:
                for (sn, v) in waits:
                    waited_vals.setdefault(sn, set()).add(v)
        for (_, sn) in self.final:
            waited_vals.setdefault(sn, set()).add(self.counts[sn])
        thresholds = {sn: sorted(vs) for sn, vs in waited_vals.items()}

        for engname in self.ENGINES:
            stream = self.streams[engname]
            fin = [(s, self.counts[s]) for (e2, s) in self.final if e2 == engname]
            if not stream and not fin:
                continue

            def run(e, stream=stream, fin=fin):
                import bisect
                cum = {}
                pend = {}
                for build, waits, semname, step in stream:
                    for (sn, v) in waits:
                        e.wait_ge(self.sems[sn], v)
                    cum[semname] = cum.get(semname, 0) + step
                    pend[semname] = pend.get(semname, 0) + step
                    inst = build(e)
                    if semname.startswith("d_"):
                        inst.then_inc(self.sems[semname], step)
                        pend[semname] = 0
                        continue
                    th = thresholds.get(semname)
                    lo = cum[semname] - pend[semname]
                    hi = cum[semname]
                    crossed = (th is not None and
                               bisect.bisect_right(th, hi) > bisect.bisect_right(th, lo))
                    if crossed:
                        inst.then_inc(self.sems[semname], pend[semname])
                        pend[semname] = 0
                for (sn, v) in fin:
                    e.wait_ge(self.sems[sn], v)

            getattr(blk, engname)(run)


def build_nc(max_phase=99):
    nc = bass.Bass("TRN2", target_bir_lowering=False)

    x_in = nc.dram_tensor("x_in", [2048, 2048], F32, kind="ExternalInput").ap()
    wqkv_in = nc.dram_tensor("wqkv_in", [16, 128, 6144], F32, kind="ExternalInput").ap()
    wout_in = nc.dram_tensor("wout_in", [2048, 2048], F32, kind="ExternalInput").ap()
    wf_in = nc.dram_tensor("wf_in", [2048, 2048], F32, kind="ExternalInput").ap()
    qg_in = nc.dram_tensor("qg_in", [1, 128], F32, kind="ExternalInput").ap()
    kg_in = nc.dram_tensor("kg_in", [1, 128], F32, kind="ExternalInput").ap()
    cos_in = nc.dram_tensor("cos_in", [2048, 128], F32, kind="ExternalInput").ap()
    sin_in = nc.dram_tensor("sin_in", [2048, 128], F32, kind="ExternalInput").ap()
    idb_in = nc.dram_tensor("idb_in", [128, 128], BF16, kind="ExternalInput").ap()
    idf_in = nc.dram_tensor("idf_in", [128, 128], F32, kind="ExternalInput").ap()
    mask_in = nc.dram_tensor("mask_in", [128, 128], F32, kind="ExternalInput").ap()
    y_out = nc.dram_tensor("y_out", [2048, 2048], F32, kind="ExternalOutput").ap()
    attnT_d = nc.dram_tensor("attnT_d", [2048, 2048], BF16, kind="Internal").ap()
    xln_d = nc.dram_tensor("xln_d", [2048, 2048], F32, kind="Internal").ap()

    es = contextlib.ExitStack()
    sb = lambda nm, shp, dt=F32: es.enter_context(nc.sbuf_tensor(nm, shp, dt))
    psb = lambda nm, shp, dt=F32: es.enter_context(nc.psum_tensor(nm, shp, dt))

    A1 = sb("A1", [128, 32768], BF16)
    A2 = sb("A2", [128, 36864], BF16)
    A3 = sb("A3", [128, 4096], F32)
    TB = sb("TB", [128, 8192], F32)
    gbc = sb("gbc", [128, 128], F32)
    idb = sb("idb", [128, 128], BF16)
    idf = sb("idf", [128, 128], F32)
    mask = sb("mask", [128, 128], F32)
    maskb = sb("maskb", [128, 128], BF16)
    qg_row = sb("qg_row", [1, 128], F32)
    kg_row = sb("kg_row", [1, 128], F32)
    qgs_row = sb("qgs_row", [1, 128], F32)
    kgs_row = sb("kgs_row", [1, 128], F32)
    ones_row = sb("ones_row", [1, 128], F32)
    ones_col = sb("ones_col", [128, 1], F32)
    c1x1 = sb("c1x1", [1, 4], F32)
    c_col = sb("c_col", [128, 1], F32)
    r127c = sb("r127c", [128, 1], F32)
    alpha_col = sb("alpha_col", [128, 1], F32)
    ralpha_col = sb("ralpha_col", [128, 1], F32)
    st_ss = sb("st_ss", [128, 64], F32)
    rq_sb = sb("rq_sb", [128, 64], F32)
    at = sb("at", [128, 16], F32)
    lnst = sb("lnst", [128, 10], F32)
    betas = sb("betas", [128, 16], F32)
    kb = sb("kb", [128, 12], F32)
    asum = sb("asum", [128, 16], F32)
    c1_t = [sb(f"c1_t{p}", [128, 128], F32) for p in range(2)]
    tmp_t = [sb(f"tmp_t{p}", [128, 128], F32) for p in range(2)]

    ps_big = psb("ps_big", [128, 3072], F32)
    ps_s = ps_big[:, 0:2048]
    ps_qkv = ps_big[:, 0:768]
    ps_wk = psb("ps_wk", [128, 512], F32)
    ps_tb_pad = psb("ps_tb_pad", [128, 1024], BF16)
    ps_tb = ps_tb_pad[:, 0:512]

    A1v = A1.rearrange("p (i t) -> p i t", i=16)
    ps_tbv = ps_tb.rearrange("p (i t) -> p i t", i=4)

    qT2_o, kT2_o, vg_o, pT_o, pb_o, wh0_o, wh1_o, sp_o = (
        0, 4096, 8192, 12288, 20480, 22528, 28672, 34816)
    xqT_o, atb_o, xqb_o = 0, 32768, 34816
    A2xq = A2.rearrange("p (i t) -> p i t", i=18)

    blk = es.enter_context(nc.Block())
    P = Prog(nc, es)
    P.max_phase = max_phase

    P.phase = 1
    P.op("sync", lambda e: e.dma_start(out=idb[:], in_=idb_in), writes=["idb"], dma="cst")
    P.op("sync", lambda e: e.dma_start(out=idf[:], in_=idf_in), writes=["idf"], dma="cs2")
    P.op("sync", lambda e: e.dma_start(out=mask[:], in_=mask_in), writes=["mask"], dma="cs3")
    P.op("gpsimd", lambda e: e.dma_start(out=maskb[:], in_=mask_in), writes=["maskb"], dma="cs6")
    P.op("sync", lambda e: e.dma_start(out=qg_row[:], in_=qg_in), writes=["qg"], dma="cs4")
    P.op("sync", lambda e: e.dma_start(out=kg_row[:], in_=kg_in), writes=["kg"], dma="cs5")
    cos_r = cos_in.rearrange("(j p) d -> p j d", p=128)
    sin_r = sin_in.rearrange("(j p) d -> p j d", p=128)
    TBv = TB.rearrange("p (s j d) -> p s j d", s=4, j=16)
    for s_i, srcr in ((0, cos_r), (1, sin_r), (2, cos_r), (3, sin_r)):
        for jc in range(0, 16, 4):
            P.op("sync", lambda e, s_i=s_i, srcr=srcr, jc=jc: e.dma_start(
                out=TBv[:, s_i, jc:jc + 4], in_=srcr[:, jc:jc + 4]),
                 writes=[f"tb{s_i}"], dma=f"tb{s_i}")
    P.op("vector", lambda e: e.memset(ones_row[:], 1.0), writes=["ones_row"])
    P.op("vector", lambda e: e.memset(ones_col[:], 1.0), writes=["ones_col"])
    P.op("vector", lambda e: e.memset(c1x1[0:1, 0:1], 128.0), writes=["c128"])
    P.op("vector", lambda e: e.memset(c1x1[0:1, 1:2], 127.0), writes=["c127"])
    P.op("scalar", lambda e: e.activation(c1x1[0:1, 2:3], c1x1[0:1, 0:1], AF.Ln),
         reads=["c128"], writes=["lnc"])
    P.op("scalar", lambda e: e.activation(c1x1[0:1, 0:1], c1x1[0:1, 2:3], AF.Exp, scale=-0.5),
         reads=["lnc"], writes=["c128"])
    P.op("vector", lambda e: e.reciprocal(c1x1[0:1, 1:2], c1x1[0:1, 1:2]),
         reads=["c127"], writes=["c127"])
    KSTEP = float(2.0 ** 46)
    P.op("vector", lambda e: e.memset(kb[:, 0:1], KSTEP), writes=["kb"])
    for li, B in enumerate((128.0, 64.0, 32.0, 16.0, 8.0, 4.0, 2.0, 1.0)):
        P.op("vector", lambda e, li=li, B=B: e.memset(kb[:, 1 + li:2 + li], -KSTEP * B),
             writes=["kb"])
    P.op("vector", lambda e: e.memset(kb[:, 9:10], -KSTEP * 0.5), writes=["kb"])
    P.op("vector", lambda e: e.memset(kb[:, 10:11], KSTEP * 0.5), writes=["kb"])
    P.op("tensor", lambda e: e.matmul(ps_qkv[:, 0:2], ones_row[0:1, :], c1x1[0:1, 0:2],
                                      start=True, stop=True),
         reads=["ones_row", "c128", "c127"], writes=["psbA"])
    P.op("vector", lambda e: e.tensor_copy(c_col[:, 0:1], ps_qkv[:, 0:1]),
         reads=["psbA"], writes=["c_col"])
    P.op("vector", lambda e: e.tensor_copy(r127c[:, 0:1], ps_qkv[:, 1:2]),
         reads=["psbA"], writes=["r127c"])
    for grow, gsrow, gk, gsk in ((qg_row, qgs_row, "qg", "qgs"),
                                 (kg_row, kgs_row, "kg", "kgs")):
        P.op("vector", lambda e, a=gsrow, b=grow: e.tensor_scalar_mul(
            a[0:1, 0:64], b[0:1, 64:128], -1.0), reads=[gk], writes=[gsk])
        P.op("vector", lambda e, a=gsrow, b=grow: e.tensor_copy(
            a[0:1, 64:128], b[0:1, 0:64]), reads=[gk], writes=[gsk])
    for s, (grow, gk) in enumerate(((qg_row, "qg"), (qgs_row, "qgs"),
                                    (kg_row, "kg"), (kgs_row, "kgs"))):
        P.op("tensor", lambda e, grow=grow: e.matmul(
            ps_qkv[:, 0:128], ones_row[0:1, :], grow[0:1, :], start=True, stop=True),
             reads=["ones_row", gk], writes=["psbA"])
        P.op("vector", lambda e: e.tensor_copy(gbc[:], ps_qkv[:, 0:128]),
             reads=["psbA"], writes=["gbc"])
        for j in range(NT):
            sl_ = TB[:, s * 2048 + j * 128: s * 2048 + (j + 1) * 128]
            P.op("vector", lambda e, sl_=sl_: e.tensor_mul(sl_, sl_, gbc[:]),
                 reads=[f"tb{s}", "gbc"], writes=[f"tb{s}"])

    P.fence()

    P.phase = 2
    for j in range(NT):
        sl = j % 2
        xb = A2[:, sl * 2048:(sl + 1) * 2048]
        P.op("gpsimd", lambda e, j=j, xb=xb: e.dma_start(
            out=xb, in_=x_in[j * 128:(j + 1) * 128, :]),
             writes=[f"xb{sl}"], dma=f"xb{sl}")
        for i0 in range(0, NI, 4):
            for di in range(4):
                i = i0 + di
                P.op("tensor", lambda e, i=i, di=di, xb=xb: e.matmul(
                    ps_tb[:, di * 128:(di + 1) * 128], xb[:, i * 128:(i + 1) * 128],
                    idb[:], start=True, stop=True, is_transpose=True),
                     reads=[f"xb{sl}", "idb"], writes=["ps_tb"])
            dst = A1v[:, i0:i0 + 4, j * 128:(j + 1) * 128]
            P.op("vector", lambda e, dst=dst: e.tensor_copy(dst, ps_tbv),
                 reads=["ps_tb"], writes=["xT"])

    P.fence()

    P.phase = 3
    qkv_o = [0, 6144]
    pT2_o = 12288
    pb2_o = 20480
    wg_o = [22528, 28672]
    sp2_o = 34816
    SC = 1024

    def emit_weights(hh):
        par = hh % 2
        P.op("gpsimd", lambda e, hh=hh, par=par: e.dma_start(
            out=A2[:, wg_o[par]: wg_o[par] + 6144], in_=wqkv_in[hh]),
             writes=[f"wg{par}"], dma=f"wg{par}")

    def stage_a_pieces(hh):
        par = hh % 2
        base = qkv_o[par]
        pieces = []
        for sj in range(4):
            for jj in range(4):
                j = sj * 4 + jj

                def piece(j=j, jj=jj, par=par, base=base):
                    for i in range(NI):
                        lhs = A1v[:, i, j * 128:(j + 1) * 128]
                        rhs = A2[:, wg_o[par] + i * 384: wg_o[par] + (i + 1) * 384]
                        P.op("tensor", lambda e, lhs=lhs, rhs=rhs, i=i: e.matmul(
                            ps_big[:, 0:384], lhs, rhs, start=(i == 0), stop=(i == NI - 1)),
                             reads=["xT", f"wg{par}"], writes=["psQ"])
                    P.op("scalar", lambda e, jj=jj: e.copy(
                        A3[:, jj * 256:(jj + 1) * 256], ps_big[:, 0:256]),
                         reads=["psQ"], writes=["qg4"])
                    P.op("scalar", lambda e, j=j, par=par, base=base: e.copy(
                        A2[:, base + 4096 + j * 128: base + 4096 + (j + 1) * 128],
                        ps_big[:, 256:384]),
                         reads=["psQ"], writes=[f"vg{par}"])
                    for qk in range(2):
                        scol = qk * 16 + j
                        srcq = A3[:, jj * 256 + qk * 128: jj * 256 + (qk + 1) * 128]
                        P.op("scalar", lambda e, srcq=srcq, scol=scol: e.activation(
                            A3[:, 2688:2816], srcq, AF.Square,
                            accum_out=st_ss[:, scol:scol + 1]),
                             reads=["qg4"], writes=["st_ss", "sqscr"])
                pieces.append(piece)

            def rope_piece(sj=sj, par=par, base=base):
                ss_v = st_ss.rearrange("p (c j) -> p c j", c=4)[:, 0:2, sj * 4:(sj + 1) * 4]
                rq_v = rq_sb.rearrange("p (c j) -> p c j", c=4)[:, 0:2, sj * 4:(sj + 1) * 4]
                P.op("vector", lambda e, ss_v=ss_v, rq_v=rq_v: e.tensor_scalar(
                    rq_v, ss_v, 1.0 / 128.0, 1e-6, ALU.mult, ALU.add),
                     reads=["st_ss"], writes=["rq"])
                P.op("scalar", lambda e, rq_v=rq_v: e.activation(rq_v, rq_v, AF.Ln),
                     reads=["rq"], writes=["rq"])
                P.op("scalar", lambda e, rq_v=rq_v: e.activation(rq_v, rq_v, AF.Exp, scale=-0.5),
                     reads=["rq"], writes=["rq"])
                for qk, tb_c, tb_s in ((0, 0, 1), (1, 2, 3)):
                    dT_o = base + qk * 2048
                    for jj in range(4):
                        j = sj * 4 + jj
                        rpar = jj % 2
                        qtile = A3[:, jj * 256 + qk * 128: jj * 256 + (qk + 1) * 128]
                        rcol = rq_sb[:, qk * 16 + j: qk * 16 + j + 1]
                        gcos = TB[:, tb_c * 2048 + j * 128: tb_c * 2048 + (j + 1) * 128]
                        gsin = TB[:, tb_s * 2048 + j * 128: tb_s * 2048 + (j + 1) * 128]
                        P.op("vector", lambda e, c1=c1_t[rpar], qtile=qtile, rcol=rcol,
                             gcos=gcos: e.scalar_tensor_tensor(
                                 c1[:], qtile, rcol, gcos, ALU.mult, ALU.mult),
                             reads=["qg4", "rq", f"tb{tb_c}"], writes=[f"c1{rpar}"])
                        P.op("vector", lambda e, tm=tmp_t[rpar], qtile=qtile, rcol=rcol,
                             gsin=gsin: e.scalar_tensor_tensor(
                                 tm[:, 0:64], qtile[:, 64:128], rcol, gsin[:, 0:64],
                                 ALU.mult, ALU.mult),
                             reads=["qg4", "rq", f"tb{tb_s}"], writes=[f"tm{rpar}"])
                        P.op("vector", lambda e, tm=tmp_t[rpar], qtile=qtile, rcol=rcol,
                             gsin=gsin: e.scalar_tensor_tensor(
                                 tm[:, 64:128], qtile[:, 0:64], rcol, gsin[:, 64:128],
                                 ALU.mult, ALU.mult),
                             reads=["qg4", "rq", f"tb{tb_s}"], writes=[f"tm{rpar}"])
                        P.op("tensor", lambda e, c1=c1_t[rpar], jj=jj: e.matmul(
                            ps_big[:, 512 + jj * 128: 512 + (jj + 1) * 128], c1[:], idf[:],
                            start=True, stop=False, is_transpose=True),
                             reads=[f"c1{rpar}", "idf"], writes=["psR"])
                        P.op("tensor", lambda e, tm=tmp_t[rpar], jj=jj: e.matmul(
                            ps_big[:, 512 + jj * 128: 512 + (jj + 1) * 128], tm[:], idf[:],
                            start=False, stop=True, is_transpose=True),
                             reads=[f"tm{rpar}", "idf"], writes=["psR"])
                    dstq = A2[:, dT_o + sj * 512: dT_o + (sj + 1) * 512]
                    P.op("scalar", lambda e, dstq=dstq: e.copy(dstq, ps_big[:, 512:1024]),
                         reads=["psR"], writes=[f"qT{par}" if qk == 0 else f"kT{par}"])
            pieces.append(rope_piece)
        return pieces

    def stage_b_pieces(hh):
        par = hh % 2
        base = qkv_o[par]
        qTk, kTk, vgk = f"qT{par}", f"kT{par}", f"vg{par}"
        pieces = []
        for c in range(4):
            def zero_piece(c=c):
                for j2 in range(3):
                    q = c * 4 + j2
                    z0 = pT2_o + j2 * 2048 + (q + 1) * 128
                    z1 = pT2_o + j2 * 2048 + (c * 4 + 4) * 128
                    P.op("vector", lambda e, z0=z0, z1=z1: e.memset(A2[:, z0:z1], 0.0),
                         writes=["pT"])
            pieces.append(zero_piece)
            for j2 in range(4):
                q = c * 4 + j2
                kt_end = (q + 1) * 128
                if kt_end <= 1024:
                    sbase = SC + (q % 2) * 1024
                    pkeys = ["psS0"] if (q % 2) == 0 else ["psS1"]
                else:
                    sbase = SC
                    pkeys = ["psS0", "psS1"]
                ato = (q % 2) * 8

                def tile_piece(q=q, kt_end=kt_end, sbase=sbase, pkeys=pkeys, ato=ato,
                               par=par, base=base, j2=j2):
                    lhs = A2[:, base + q * 128: base + (q + 1) * 128]
                    ncc = (kt_end + 511) // 512
                    for cc in range(ncc):
                        c0, c1c = cc * 512, min(kt_end, (cc + 1) * 512)
                        rhs = A2[:, base + 2048 + c0: base + 2048 + c1c]
                        last = (cc == ncc - 1)
                        P.op("tensor", lambda e, lhs=lhs, rhs=rhs, c0=c0, c1c=c1c, last=last:
                             e.matmul(ps_big[:, sbase + c0: sbase + c1c], lhs, rhs,
                                      start=True, stop=not last),
                             reads=[qTk, kTk], writes=pkeys)
                        if last:
                            P.op("tensor", lambda e: e.matmul(
                                ps_big[:, sbase + q * 128: sbase + kt_end],
                                idb[:], maskb[:], start=False, stop=True),
                                 reads=["idb", "maskb"] + pkeys, writes=pkeys)
                        P.op("vector", lambda e, cc=cc, c0=c0, c1c=c1c: e.reduce_max(
                            at[:, ato + 4 + cc: ato + 5 + cc],
                            ps_big[:, sbase + c0: sbase + c1c], axis=AX, negate=True),
                             reads=pkeys, writes=[f"atm{ato}{cc}"])
                    P.op("vector", lambda e, ncc=ncc: e.tensor_reduce(
                        at[:, ato:ato + 1], at[:, ato + 4: ato + 4 + ncc], axis=AX,
                        op=ALU.min),
                         reads=[f"atm{ato}{cc2}" for cc2 in range(ncc)], writes=[f"at0{ato}"])
                    P.op("vector", lambda e: e.tensor_mul(
                        at[:, ato + 1:ato + 2], at[:, ato:ato + 1], c_col[:]),
                         reads=[f"at0{ato}", "c_col"], writes=[f"at1{ato}"])
                    nh = 2 if kt_end > 512 else 1
                    hbs = [(0, kt_end // nh if nh == 2 else kt_end), (kt_end // nh, kt_end)]
                    for hfi in range(nh):
                        hb0, hb1 = hbs[hfi] if nh == 2 else (0, kt_end)
                        P.op("scalar", lambda e, hb0=hb0, hb1=hb1, hfi=hfi: e.activation(
                            A2[:, pb2_o + hb0: pb2_o + hb1],
                            ps_big[:, sbase + hb0: sbase + hb1], AF.Exp,
                            bias=at[:, ato + 1:ato + 2], scale=c_col[:, 0:1],
                            accum_out=at[:, ato + 4 + hfi: ato + 5 + hfi]),
                             reads=pkeys + [f"at1{ato}", "c_col"],
                             writes=[f"pbh{hfi}", f"atm{ato}{hfi}"])
                    if nh == 2:
                        P.op("vector", lambda e: e.tensor_add(
                            at[:, ato + 2:ato + 3], at[:, ato + 4:ato + 5],
                            at[:, ato + 5:ato + 6]),
                             reads=[f"atm{ato}0", f"atm{ato}1"], writes=[f"at2{ato}"])
                    else:
                        P.op("vector", lambda e: e.tensor_copy(
                            at[:, ato + 2:ato + 3], at[:, ato + 4:ato + 5]),
                             reads=[f"atm{ato}0"], writes=[f"at2{ato}"])
                    P.op("vector", lambda e: e.reciprocal(
                        at[:, ato + 3:ato + 4], at[:, ato + 2:ato + 3]),
                         reads=[f"at2{ato}"], writes=[f"at3{ato}"])
                pieces.append(tile_piece)

                def tr_piece(q=q, kt_end=kt_end, sbase=sbase, pkeys=pkeys, ato=ato,
                             par=par, base=base, j2=j2):
                    for i0 in range(0, q + 1, 4):
                        nb = min(4, q + 1 - i0)
                        P.op("vector", lambda e, i0=i0, nb=nb: e.tensor_scalar_mul(
                            A2[:, pb2_o + i0 * 128: pb2_o + (i0 + nb) * 128],
                            A2[:, pb2_o + i0 * 128: pb2_o + (i0 + nb) * 128],
                            at[:, ato + 3:ato + 4]),
                             reads=["pbh0", "pbh1", f"at3{ato}"], writes=["pbs"])
                        for di in range(nb):
                            i = i0 + di
                            P.op("tensor", lambda e, i=i, di=di: e.matmul(
                                ps_tb[:, di * 128:(di + 1) * 128],
                                A2[:, pb2_o + i * 128: pb2_o + (i + 1) * 128],
                                idb[:], start=True, stop=True, is_transpose=True),
                                 reads=["pbs", "idb"], writes=["ps_tb"])
                        dst = A2[:, pT2_o + j2 * 2048 + i0 * 128:
                                 pT2_o + j2 * 2048 + (i0 + nb) * 128]
                        P.op("scalar", lambda e, dst=dst, nb=nb: e.copy(
                            dst, ps_tb[:, 0:nb * 128]),
                             reads=["ps_tb"], writes=["pT"])
                pieces.append(tr_piece)

            def av_piece(c=c, par=par, base=base):
                nkt = c * 4 + 4
                pT3 = A2[:, pT2_o: pT2_o + 8192].rearrange("p (j2 t) -> p j2 t", j2=4)
                for i in range(nkt):
                    lhs = A2[:, base + 4096 + i * 128: base + 4096 + (i + 1) * 128]
                    rhs = pT3[:, :, i * 128:(i + 1) * 128]
                    P.op("tensor", lambda e, lhs=lhs, rhs=rhs, i=i, nkt=nkt: e.matmul(
                        ps_wk[:], lhs, rhs, start=(i == 0), stop=(i == nkt - 1)),
                         reads=[vgk, "pT"], writes=["ps_wk"])
                P.op("scalar", lambda e, c=c: e.copy(
                    A2[:, sp2_o + c * 512: sp2_o + (c + 1) * 512], ps_wk[:]),
                     reads=["ps_wk"], writes=["atst"])
            pieces.append(av_piece)

        def store_piece(hh=hh):
            P.op("sync", lambda e, hh=hh: e.dma_start(
                out=attnT_d[hh * 128:(hh + 1) * 128, :], in_=A2[:, sp2_o: sp2_o + 2048]),
                 reads=["atst"], writes=[f"attnT{hh}"], dma="atst")
        pieces.append(store_piece)
        return pieces

    emit_weights(0)
    prevB = None
    for hh in range(NH + 1):
        if hh < NH:
            if hh + 1 < NH:
                emit_weights(hh + 1)
            pa = list(stage_a_pieces(hh))
        else:
            pa = []
        pb_l = prevB if prevB is not None else []
        if not pb_l:
            for p_ in pa:
                p_()
        else:
            ai = 0
            for c in range(4):
                basei = c * 10
                pb_l[basei]()
                for j2 in range(4):
                    pb_l[basei + 1 + 2 * j2]()
                    if ai < len(pa):
                        pa[ai]()
                        ai += 1
                    pb_l[basei + 2 + 2 * j2]()
                pb_l[basei + 9]()
                if ai < len(pa):
                    pa[ai]()
                    ai += 1
            pb_l[40]()
            while ai < len(pa):
                pa[ai]()
                ai += 1
        prevB = stage_b_pieces(hh) if hh < NH else None

    P.fence()

    P.phase = 4
    for i in range(NI):
        P.op("gpsimd", lambda e, i=i: e.dma_start(
            out=A1[:, i * 2048:(i + 1) * 2048], in_=wout_in[i * 128:(i + 1) * 128, :]),
             writes=["wout"], dma="wout")
    pending_tr = [None]

    def p3_tile(j):
        sl = j % 2
        atb_base = atb_o
        atbv = A2[:, atb_base: atb_base + 2048].rearrange("p (i t) -> p i t", i=16)
        atsv = attnT_d[:, j * 128:(j + 1) * 128].rearrange("(i p) t -> p i t", p=128)
        for ic in range(0, 16, 4):
            P.op("scalar", lambda e, atbv=atbv, atsv=atsv, ic=ic, j=j: e.dma_start(
                out=atbv[:, ic:ic + 4], in_=atsv[:, ic:ic + 4]),
                 writes=["atb0"], dma="atb0")
        P.op("sync", lambda e, j=j, sl=sl: e.dma_start(
            out=TB[:, sl * 2048:(sl + 1) * 2048], in_=x_in[j * 128:(j + 1) * 128, :]),
             writes=[f"xres{sl}"], dma=f"xres{sl}")
        for i in range(NI):
            lhs = A2[:, atb_base + i * 128: atb_base + (i + 1) * 128]
            for cc in range(4):
                rhs = A1v[:, i, cc * 512:(cc + 1) * 512]
                P.op("tensor", lambda e, lhs=lhs, rhs=rhs, i=i, cc=cc: e.matmul(
                    ps_s[:, cc * 512:(cc + 1) * 512], lhs, rhs,
                    start=(i == 0), stop=(i == NI - 1)),
                     reads=["atb0", "wout"], writes=["psbA", "psbB"])
        if pending_tr[0] is not None:
            pending_tr[0]()
            pending_tr[0] = None

        P.op("vector", lambda e, sl=sl: e.tensor_add(
            TB[:, 4096:6144], ps_s[:], TB[:, sl * 2048:(sl + 1) * 2048]),
             reads=["psbA", "psbB", f"xres{sl}"], writes=["xr"])
        _ln_core(P, TB, A3, lnst)
        P.op("sync", lambda e, j=j: e.dma_start(
            out=xln_d[j * 128:(j + 1) * 128, :], in_=A3[:, 0:2048]),
             reads=["xln"], writes=[f"xln_d{j}"], dma="xlnst")
        P.op("vector", lambda e: e.reduce_max(lnst[:, 6:7], A3[:, 0:2048], axis=AX,
                                              apply_absolute_value=True),
             reads=["xln"], writes=["bm"])
        P.op("vector", lambda e: e.tensor_mul(lnst[:, 7:8], lnst[:, 6:7], r127c[:]),
             reads=["bm", "r127c"], writes=["b1"])
        P.op("vector", lambda e, j=j: e.tensor_scalar_max(
            betas[:, j:j + 1], lnst[:, 7:8], 1e-8),
             reads=["b1"], writes=["betas"])
        P.op("vector", lambda e, j=j: e.reciprocal(lnst[:, 8:9], betas[:, j:j + 1]),
             reads=["betas"], writes=["rbeta"])
        P.op("vector", lambda e: e.tensor_scalar(
            TB[:, 6144:8192], A3[:, 0:2048], lnst[:, 8:9], 128.5, ALU.mult, ALU.add),
             reads=["xln", "rbeta"], writes=["t2", "t20", "t21"])
        for li, B in enumerate((128.0, 64.0, 32.0, 16.0, 8.0, 4.0, 2.0, 1.0)):
            for hf in range(2):
                h0c, h1c = hf * 1024, (hf + 1) * 1024
                vsrc = TB[:, 6144 + h0c: 6144 + h1c] if li == 0 else TB[:, 4096 + h0c: 4096 + h1c]
                vkey = f"t2{hf}" if li == 0 else f"modm{hf}"
                P.op("scalar", lambda e, vsrc=vsrc, li=li, h0c=h0c, h1c=h1c: e.activation(
                    A3[:, 2048 + h0c: 2048 + h1c], vsrc, AF.Sigmoid,
                    scale=kb[:, 0:1], bias=kb[:, 1 + li:2 + li]),
                     reads=[vkey, "kb"], writes=[f"bq{hf}"])
                P.op("vector", lambda e, vsrc=vsrc, B=B, h0c=h0c, h1c=h1c: e.scalar_tensor_tensor(
                    TB[:, 4096 + h0c: 4096 + h1c], A3[:, 2048 + h0c: 2048 + h1c], -B, vsrc,
                    ALU.mult, ALU.add),
                     reads=[f"bq{hf}", vkey], writes=[f"modm{hf}"])
        P.op("vector", lambda e: e.tensor_sub(
            A3[:, 2048:4096], TB[:, 6144:8192], TB[:, 4096:6144]),
             reads=["t2", "modm0", "modm1", "bq0", "bq1"], writes=["xqf"])
        P.op("vector", lambda e: e.tensor_scalar(
            A3[:, 2048:4096], A3[:, 2048:4096], -128.0, -127.0, ALU.add, ALU.max),
             reads=["xqf"], writes=["xqf"])
        P.op("vector", lambda e: e.tensor_scalar(
            A2[:, xqb_o: xqb_o + 2048], A3[:, 2048:4096], 127.0, None, ALU.min),
             reads=["xqf"], writes=["xqb"])

        def tr(j=j):
            for i0 in range(0, NI, 4):
                for di in range(4):
                    i = i0 + di
                    P.op("tensor", lambda e, i=i, di=di: e.matmul(
                        ps_tb[:, di * 128:(di + 1) * 128],
                        A2[:, xqb_o + i * 128: xqb_o + (i + 1) * 128],
                        idb[:], start=True, stop=True, is_transpose=True),
                         reads=["xqb", "idb"], writes=["ps_tb"])
                dst = A2xq[:, i0:i0 + 4, j * 128:(j + 1) * 128]
                P.op("vector", lambda e, dst=dst: e.tensor_copy(dst, ps_tbv),
                     reads=["ps_tb"], writes=["xqT"])
        pending_tr[0] = tr

    for j in range(NT):
        p3_tile(j)
    pending_tr[0]()
    pending_tr[0] = None

    P.fence()

    P.phase = 5
    for r in range(NI):
        sl = r % 2
        P.op("sync", lambda e, r=r, sl=sl: e.dma_start(
            out=TB[:, sl * 2048:(sl + 1) * 2048], in_=wf_in[r * 128:(r + 1) * 128, :]),
             writes=[f"xres{sl}"], dma=f"xres{sl}")
        P.op("vector", lambda e, r=r, sl=sl: e.reduce_sum(
            asum[:, r:r + 1], TB[:, sl * 2048:(sl + 1) * 2048], axis=AX,
            apply_absolute_value=True),
             reads=[f"xres{sl}"], writes=["asum"])
    P.op("vector", lambda e: e.reduce_sum(lnst[:, 0:1], asum[:], axis=AX),
         reads=["asum"], writes=["ascol"])
    P.op("tensor", lambda e: e.matmul(ps_qkv[0:1, 0:1], ones_col[:, 0:1], lnst[:, 0:1],
                                      start=True, stop=True),
         reads=["ones_col", "ascol"], writes=["psbA"])
    P.op("vector", lambda e: e.tensor_scalar(
        c1x1[0:1, 3:4], ps_qkv[0:1, 0:1], 1.0 / 4194304.0, 1e-8, ALU.mult, ALU.max),
         reads=["psbA"], writes=["alpha1"])
    P.op("vector", lambda e: e.reciprocal(c1x1[0:1, 2:3], c1x1[0:1, 3:4]),
         reads=["alpha1"], writes=["ralpha1"])
    P.op("tensor", lambda e: e.matmul(ps_qkv[:, 0:2], ones_row[0:1, :], c1x1[0:1, 2:4],
                                      start=True, stop=True),
         reads=["ones_row", "alpha1", "ralpha1"], writes=["psbA"])
    P.op("vector", lambda e: e.tensor_copy(ralpha_col[:, 0:1], ps_qkv[:, 0:1]),
         reads=["psbA"], writes=["ralpha_col"])
    P.op("vector", lambda e: e.tensor_copy(alpha_col[:, 0:1], ps_qkv[:, 1:2]),
         reads=["psbA"], writes=["alpha_col"])
    for r in range(NI):
        sl = r % 2
        P.op("sync", lambda e, r=r, sl=sl: e.dma_start(
            out=TB[:, sl * 2048:(sl + 1) * 2048], in_=wf_in[r * 128:(r + 1) * 128, :]),
             writes=[f"xres{sl}"], dma=f"xres{sl}")
        wsrc = TB[:, sl * 2048:(sl + 1) * 2048]
        P.op("vector", lambda e, wsrc=wsrc: e.tensor_scalar(
            TB[:, 4096:6144], wsrc, ralpha_col[:, 0:1], None, ALU.mult),
             reads=[f"xres{sl}", "ralpha_col"], writes=["t2"])
        P.op("scalar", lambda e: e.activation(
            A3[:, 0:2048], TB[:, 4096:6144], AF.Sigmoid,
            scale=kb[:, 0:1], bias=kb[:, 9:10]),
             reads=["t2", "kb"], writes=["b1"])
        P.op("scalar", lambda e: e.activation(
            A3[:, 2048:4096], TB[:, 4096:6144], AF.Sigmoid,
            scale=kb[:, 0:1], bias=kb[:, 10:11]),
             reads=["t2", "kb"], writes=["b2"])
        P.op("vector", lambda e: e.tensor_add(
            TB[:, 6144:8192], A3[:, 0:2048], A3[:, 2048:4096]),
             reads=["b1", "b2"], writes=["s12"])
        P.op("vector", lambda e: e.tensor_scalar(
            A2[:, atb_o: atb_o + 2048], TB[:, 6144:8192], -1.0, None, ALU.add),
             reads=["s12"], writes=["wqb"])
        for i0 in range(0, NI, 4):
            for di in range(4):
                i = i0 + di
                P.op("tensor", lambda e, i=i, di=di: e.matmul(
                    ps_tb[:, di * 128:(di + 1) * 128],
                    A2[:, atb_o + i * 128: atb_o + (i + 1) * 128],
                    idb[:], start=True, stop=True, is_transpose=True),
                     reads=["wqb", "idb"], writes=["ps_tb"])
            dst = A1v[:, i0:i0 + 4, r * 128:(r + 1) * 128]
            P.op("vector", lambda e, dst=dst: e.tensor_copy(dst, ps_tbv),
                 reads=["ps_tb"], writes=["wqT"])

    P.fence()

    P.phase = 6
    for j in range(NT):
        sl = j % 2
        P.op("sync", lambda e, j=j, sl=sl: e.dma_start(
            out=TB[:, sl * 2048:(sl + 1) * 2048], in_=xln_d[j * 128:(j + 1) * 128, :]),
             writes=[f"xres{sl}"], dma=f"xres{sl}")
        for i in range(NI):
            lhs = A2xq[:, i, j * 128:(j + 1) * 128]
            for cc in range(4):
                rhs = A1v[:, i, cc * 512:(cc + 1) * 512]
                P.op("tensor", lambda e, lhs=lhs, rhs=rhs, i=i, cc=cc: e.matmul(
                    ps_s[:, cc * 512:(cc + 1) * 512], lhs, rhs,
                    start=(i == 0), stop=(i == NI - 1)),
                     reads=["xqT", "wqT"], writes=["psbA", "psbB"])
        P.op("vector", lambda e, j=j: e.tensor_mul(
            lnst[:, 9:10], alpha_col[:], betas[:, j:j + 1]),
             reads=["alpha_col", "betas"], writes=["abcol"])
        P.op("vector", lambda e, sl=sl: e.scalar_tensor_tensor(
            TB[:, 4096:6144], ps_s[:], lnst[:, 9:10], TB[:, sl * 2048:(sl + 1) * 2048],
            ALU.mult, ALU.add),
             reads=["psbA", "psbB", "abcol", f"xres{sl}"], writes=["xr"])
        _ln_core(P, TB, A3, lnst)
        P.op("sync", lambda e, j=j: e.dma_start(
            out=y_out[j * 128:(j + 1) * 128, :], in_=A3[:, 0:2048]),
             reads=["xln"], writes=[f"y{j}"], dma="yout")

    if max_phase < 6:
        P.phase = 0
        P.fence()
        P.op("vector", lambda e: e.memset(A3[:, 0:2048], 0.0), writes=["dummy"])
        for j in range(NT):
            P.op("sync", lambda e, j=j: e.dma_start(
                out=y_out[j * 128:(j + 1) * 128, :], in_=A3[:, 0:2048]),
                 reads=["dummy"], writes=[f"dy{j}"], dma="yout")
    P.final.append(("sync", "d_yout"))
    P.emit(blk)
    es.close()
    return nc


def _ln_core(P, TB, A3, lnst):
    P.op("vector", lambda e: e.reduce_sum(lnst[:, 0:1], TB[:, 4096:6144], axis=AX, negate=True),
         reads=["xr"], writes=["nsum"])
    P.op("vector", lambda e: e.tensor_scalar_mul(lnst[:, 1:2], lnst[:, 0:1], 1.0 / 2048.0),
         reads=["nsum"], writes=["nmean"])
    P.op("scalar", lambda e: e.activation(A3[:, 2048:4096], TB[:, 4096:6144], AF.Square,
                                          bias=lnst[:, 1:2], scale=1.0,
                                          accum_out=lnst[:, 2:3]),
         reads=["xr", "nmean"], writes=["sumsq", "xqf"])
    P.op("vector", lambda e: e.tensor_scalar(lnst[:, 3:4], lnst[:, 2:3],
                                             1.0 / 2048.0, 1e-5, ALU.mult, ALU.add),
         reads=["sumsq"], writes=["varep"])
    P.op("scalar", lambda e: e.activation(lnst[:, 4:5], lnst[:, 3:4], AF.Ln),
         reads=["varep"], writes=["lnv"])
    P.op("scalar", lambda e: e.activation(lnst[:, 5:6], lnst[:, 4:5], AF.Exp, scale=-0.5),
         reads=["lnv"], writes=["rstd"])
    P.op("vector", lambda e: e.tensor_scalar(A3[:, 0:2048], TB[:, 4096:6144],
                                             lnst[:, 1:2], lnst[:, 5:6], ALU.add, ALU.mult),
         reads=["xr", "nmean", "rstd"], writes=["xln"])


def _rope_tables():
    try:
        import jax
        import jax.numpy as jnp
        with jax.default_device(jax.devices("cpu")[0]):
            inv = 1.0 / 10000.0 ** (jnp.arange(0, HD, 2, dtype=jnp.float32) / HD)
            t = jnp.arange(2048, dtype=jnp.float32)
            fr = t[:, None] * inv[None, :]
            emb = jnp.concatenate((fr, fr), axis=-1)
            return (np.asarray(jnp.cos(emb), dtype=np.float32),
                    np.asarray(jnp.sin(emb), dtype=np.float32))
    except Exception:
        inv = (1.0 / np.float32(10000.0) ** (np.arange(0, HD, 2, dtype=np.float32)
                                             / np.float32(HD))).astype(np.float32)
        t = np.arange(2048, dtype=np.float32)
        fr = (t[:, None] * inv[None, :]).astype(np.float32)
        emb = np.concatenate((fr, fr), axis=-1)
        return np.cos(emb).astype(np.float32), np.sin(emb).astype(np.float32)


_NC_CACHE = [None]


def kernel(L0_attn_x, L0_attn_w_qkv, L0_attn_w_out, L0_attn_q_gamma,
           L0_attn_k_gamma, L0_ffn_weight):
    if _NC_CACHE[0] is None:
        _NC_CACHE[0] = build_nc()
    nc = _NC_CACHE[0]

    cos_t, sin_t = _rope_tables()
    tri = np.tril(np.ones((128, 128), dtype=bool))
    mask = np.where(tri, np.float32(0.0), np.float32(MASKVAL)).astype(np.float32)
    idb = np.eye(128, dtype=ml_dtypes.bfloat16)
    idf = np.eye(128, dtype=np.float32)
    qg = np.ascontiguousarray(np.asarray(L0_attn_q_gamma, dtype=np.float32).reshape(1, 128))
    kg = np.ascontiguousarray(np.asarray(L0_attn_k_gamma, dtype=np.float32).reshape(1, 128))
    _w = np.asarray(L0_attn_w_qkv, dtype=np.float32).reshape(16, 128, 48, 128)
    wqkv = np.ascontiguousarray(
        np.stack([np.concatenate([_w[:, :, h], _w[:, :, 16 + h], _w[:, :, 32 + h]],
                                 axis=-1) for h in range(16)])
        .transpose(0, 2, 1, 3)
        .reshape(16, 128, 6144))
    wout = np.ascontiguousarray(np.asarray(L0_attn_w_out, dtype=np.float32))
    wf = np.ascontiguousarray(np.asarray(L0_ffn_weight, dtype=np.float32))
    x = np.asarray(L0_attn_x, dtype=np.float32)

    in_maps = []
    for c in range(NCORES):
        in_maps.append({
            "x_in": np.ascontiguousarray(x[c]),
            "wqkv_in": wqkv, "wout_in": wout, "wf_in": wf,
            "qg_in": qg, "kg_in": kg,
            "cos_in": cos_t, "sin_in": sin_t,
            "idb_in": idb, "idf_in": idf, "mask_in": mask,
        })
    res = run_bass_kernel_spmd(nc, in_maps, core_ids=list(range(NCORES)))
    out = np.stack([np.asarray(r["y_out"], dtype=np.float32) for r in res.results], axis=0)
    return out


# version 24
# speedup vs baseline: 1.0736x; 1.0736x over previous
import contextlib
import numpy as np
import ml_dtypes

import concourse.bass as bass
import concourse.mybir as mybir
from concourse.bass_utils import run_bass_kernel_spmd

F32 = mybir.dt.float32
BF16 = mybir.dt.bfloat16
AF = mybir.ActivationFunctionType
ALU = mybir.AluOpType
AX = mybir.AxisListType.X

NT = 16
NI = 16
NH = 16
HD = 128
H = 2048
NCORES = 8

MASKVAL = float(np.float32(-1e30 * np.sqrt(128.0)))


def part_col(part, hh):
    return {"q": 0, "k": 2048, "v": 4096}[part] + hh * 128


class Prog:
    ENGINES = ("sync", "gpsimd", "scalar", "vector", "tensor")

    def __init__(self, nc, es):
        self.nc = nc
        self.es = es
        self.streams = {e: [] for e in self.ENGINES}
        self.sems = {}
        self.counts = {}
        self.writer = {}
        self.readers = {}
        self.waited = {e: {} for e in self.ENGINES}
        self.fence_mark = None
        self.final = []
        self.phase = 0
        self.max_phase = 99

    def _sem(self, name):
        if name not in self.sems:
            self.sems[name] = self.es.enter_context(self.nc.semaphore(name))
            self.counts[name] = 0
        return self.sems[name]

    def op(self, eng, build, reads=(), writes=(), dma=None):
        if self.phase > self.max_phase:
            return
        waits = []
        w = self.waited[eng]

        def need(semname, val):
            if semname == "c_tensor" and eng == "tensor":
                return
            if w.get(semname, 0) < val:
                waits.append((semname, val))
                w[semname] = val

        if self.fence_mark is not None:
            need(*self.fence_mark)
        for k in reads:
            wr = self.writer.get(k)
            if wr is not None:
                need(*wr)
        for k in writes:
            wr = self.writer.get(k)
            if wr is not None:
                need(*wr)
            for sv in self.readers.get(k, ()):
                need(*sv)
        if dma is not None:
            semname, step = "d_" + dma, 16
        else:
            semname, step = "c_" + eng, 1
        self._sem(semname)
        self.counts[semname] += step
        val = self.counts[semname]
        self.streams[eng].append((build, waits, semname, step))
        for k in writes:
            self.writer[k] = (semname, val)
            self.readers[k] = []
        for k in reads:
            self.readers.setdefault(k, []).append((semname, val))

    def fence(self):
        marks = {}
        if self.fence_mark is not None:
            marks[self.fence_mark[0]] = self.fence_mark[1]
        for wr in self.writer.values():
            marks[wr[0]] = max(marks.get(wr[0], 0), wr[1])
        for rl in self.readers.values():
            for (s, v) in rl:
                marks[s] = max(marks.get(s, 0), v)
        waits = [(s, v) for s, v in marks.items()]
        self._sem("c_vector")
        self.counts["c_vector"] += 1
        val = self.counts["c_vector"]
        self.streams["vector"].append((lambda e: e.engine_nop(), waits, "c_vector", 1))
        self.fence_mark = ("c_vector", val)
        self.writer = {}
        self.readers = {}
        self.waited = {e: {} for e in self.ENGINES}

    def emit(self, blk):
        waited_vals = {}
        for st in self.streams.values():
            for _, waits, _, _ in st:
                for (sn, v) in waits:
                    waited_vals.setdefault(sn, set()).add(v)
        for (_, sn) in self.final:
            waited_vals.setdefault(sn, set()).add(self.counts[sn])
        thresholds = {sn: sorted(vs) for sn, vs in waited_vals.items()}

        for engname in self.ENGINES:
            stream = self.streams[engname]
            fin = [(s, self.counts[s]) for (e2, s) in self.final if e2 == engname]
            if not stream and not fin:
                continue

            def run(e, stream=stream, fin=fin):
                import bisect
                cum = {}
                pend = {}
                for build, waits, semname, step in stream:
                    for (sn, v) in waits:
                        e.wait_ge(self.sems[sn], v)
                    cum[semname] = cum.get(semname, 0) + step
                    pend[semname] = pend.get(semname, 0) + step
                    inst = build(e)
                    if semname.startswith("d_"):
                        inst.then_inc(self.sems[semname], step)
                        pend[semname] = 0
                        continue
                    th = thresholds.get(semname)
                    lo = cum[semname] - pend[semname]
                    hi = cum[semname]
                    crossed = (th is not None and
                               bisect.bisect_right(th, hi) > bisect.bisect_right(th, lo))
                    if crossed:
                        inst.then_inc(self.sems[semname], pend[semname])
                        pend[semname] = 0
                for (sn, v) in fin:
                    e.wait_ge(self.sems[sn], v)

            getattr(blk, engname)(run)


def build_nc(max_phase=99):
    nc = bass.Bass("TRN2", target_bir_lowering=False)

    x_in = nc.dram_tensor("x_in", [2048, 2048], F32, kind="ExternalInput").ap()
    wqkv_in = nc.dram_tensor("wqkv_in", [16, 128, 6144], F32, kind="ExternalInput").ap()
    wout_in = nc.dram_tensor("wout_in", [2048, 2048], F32, kind="ExternalInput").ap()
    wf_in = nc.dram_tensor("wf_in", [2048, 2048], F32, kind="ExternalInput").ap()
    qg_in = nc.dram_tensor("qg_in", [1, 128], F32, kind="ExternalInput").ap()
    kg_in = nc.dram_tensor("kg_in", [1, 128], F32, kind="ExternalInput").ap()
    cos_in = nc.dram_tensor("cos_in", [2048, 128], F32, kind="ExternalInput").ap()
    sin_in = nc.dram_tensor("sin_in", [2048, 128], F32, kind="ExternalInput").ap()
    idb_in = nc.dram_tensor("idb_in", [128, 128], BF16, kind="ExternalInput").ap()
    idf_in = nc.dram_tensor("idf_in", [128, 128], F32, kind="ExternalInput").ap()
    mask_in = nc.dram_tensor("mask_in", [128, 128], F32, kind="ExternalInput").ap()
    y_out = nc.dram_tensor("y_out", [2048, 2048], F32, kind="ExternalOutput").ap()
    attnT_d = nc.dram_tensor("attnT_d", [2048, 2048], BF16, kind="Internal").ap()
    xln_d = nc.dram_tensor("xln_d", [2048, 2048], F32, kind="Internal").ap()

    es = contextlib.ExitStack()
    sb = lambda nm, shp, dt=F32: es.enter_context(nc.sbuf_tensor(nm, shp, dt))
    psb = lambda nm, shp, dt=F32: es.enter_context(nc.psum_tensor(nm, shp, dt))

    A1 = sb("A1", [128, 32768], BF16)
    A2 = sb("A2", [128, 36864], BF16)
    A3 = sb("A3", [128, 4096], F32)
    TB = sb("TB", [128, 8192], F32)
    gbc = sb("gbc", [128, 128], F32)
    idb = sb("idb", [128, 128], BF16)
    idf = sb("idf", [128, 128], F32)
    mask = sb("mask", [128, 128], F32)
    maskb = sb("maskb", [128, 128], BF16)
    qg_row = sb("qg_row", [1, 128], F32)
    kg_row = sb("kg_row", [1, 128], F32)
    qgs_row = sb("qgs_row", [1, 128], F32)
    kgs_row = sb("kgs_row", [1, 128], F32)
    ones_row = sb("ones_row", [1, 128], F32)
    ones_col = sb("ones_col", [128, 1], F32)
    c1x1 = sb("c1x1", [1, 4], F32)
    c_col = sb("c_col", [128, 1], F32)
    r127c = sb("r127c", [128, 1], F32)
    alpha_col = sb("alpha_col", [128, 1], F32)
    ralpha_col = sb("ralpha_col", [128, 1], F32)
    st_ss = sb("st_ss", [128, 64], F32)
    rq_sb = sb("rq_sb", [128, 64], F32)
    at = sb("at", [128, 16], F32)
    lnst = sb("lnst", [128, 10], F32)
    betas = sb("betas", [128, 16], F32)
    kb = sb("kb", [128, 12], F32)
    asum = sb("asum", [128, 16], F32)
    c1_t = [sb(f"c1_t{p}", [128, 128], F32) for p in range(2)]
    tmp_t = [sb(f"tmp_t{p}", [128, 128], F32) for p in range(2)]

    ps_big = psb("ps_big", [128, 3072], F32)
    ps_s = ps_big[:, 0:2048]
    ps_qkv = ps_big[:, 0:768]
    ps_wk = psb("ps_wk", [128, 512], F32)
    ps_tb_pad = psb("ps_tb_pad", [128, 1024], BF16)
    ps_tb = ps_tb_pad[:, 0:512]

    A1v = A1.rearrange("p (i t) -> p i t", i=16)
    ps_tbv = ps_tb.rearrange("p (i t) -> p i t", i=4)

    qT2_o, kT2_o, vg_o, pT_o, pb_o, wh0_o, wh1_o, sp_o = (
        0, 4096, 8192, 12288, 20480, 22528, 28672, 34816)
    xqT_o, atb_o, xqb_o = 0, 32768, 34816
    A2xq = A2.rearrange("p (i t) -> p i t", i=18)

    blk = es.enter_context(nc.Block())
    P = Prog(nc, es)
    P.max_phase = max_phase

    P.phase = 1
    P.op("sync", lambda e: e.dma_start(out=idb[:], in_=idb_in), writes=["idb"], dma="cst")
    P.op("sync", lambda e: e.dma_start(out=idf[:], in_=idf_in), writes=["idf"], dma="cs2")
    P.op("sync", lambda e: e.dma_start(out=mask[:], in_=mask_in), writes=["mask"], dma="cs3")
    P.op("gpsimd", lambda e: e.dma_start(out=maskb[:], in_=mask_in), writes=["maskb"], dma="cs6")
    P.op("sync", lambda e: e.dma_start(out=qg_row[:], in_=qg_in), writes=["qg"], dma="cs4")
    P.op("sync", lambda e: e.dma_start(out=kg_row[:], in_=kg_in), writes=["kg"], dma="cs5")
    cos_r = cos_in.rearrange("(j p) d -> p j d", p=128)
    sin_r = sin_in.rearrange("(j p) d -> p j d", p=128)
    TBv = TB.rearrange("p (s j d) -> p s j d", s=4, j=16)
    for s_i, srcr in ((0, cos_r), (1, sin_r), (2, cos_r), (3, sin_r)):
        for jc in range(0, 16, 4):
            P.op("sync", lambda e, s_i=s_i, srcr=srcr, jc=jc: e.dma_start(
                out=TBv[:, s_i, jc:jc + 4], in_=srcr[:, jc:jc + 4]),
                 writes=[f"tb{s_i}"], dma=f"tb{s_i}")
    P.op("vector", lambda e: e.memset(ones_row[:], 1.0), writes=["ones_row"])
    P.op("vector", lambda e: e.memset(ones_col[:], 1.0), writes=["ones_col"])
    P.op("vector", lambda e: e.memset(c1x1[0:1, 0:1], 128.0), writes=["c128"])
    P.op("vector", lambda e: e.memset(c1x1[0:1, 1:2], 127.0), writes=["c127"])
    P.op("scalar", lambda e: e.activation(c1x1[0:1, 2:3], c1x1[0:1, 0:1], AF.Ln),
         reads=["c128"], writes=["lnc"])
    P.op("scalar", lambda e: e.activation(c1x1[0:1, 0:1], c1x1[0:1, 2:3], AF.Exp, scale=-0.5),
         reads=["lnc"], writes=["c128"])
    P.op("vector", lambda e: e.reciprocal(c1x1[0:1, 1:2], c1x1[0:1, 1:2]),
         reads=["c127"], writes=["c127"])
    KSTEP = float(2.0 ** 46)
    P.op("vector", lambda e: e.memset(kb[:, 0:1], KSTEP), writes=["kb"])
    for li, B in enumerate((128.0, 64.0, 32.0, 16.0, 8.0, 4.0, 2.0, 1.0)):
        P.op("vector", lambda e, li=li, B=B: e.memset(kb[:, 1 + li:2 + li], -KSTEP * B),
             writes=["kb"])
    P.op("vector", lambda e: e.memset(kb[:, 9:10], -KSTEP * 0.5), writes=["kb"])
    P.op("vector", lambda e: e.memset(kb[:, 10:11], KSTEP * 0.5), writes=["kb"])
    P.op("tensor", lambda e: e.matmul(ps_qkv[:, 0:2], ones_row[0:1, :], c1x1[0:1, 0:2],
                                      start=True, stop=True),
         reads=["ones_row", "c128", "c127"], writes=["psbA"])
    P.op("vector", lambda e: e.tensor_copy(c_col[:, 0:1], ps_qkv[:, 0:1]),
         reads=["psbA"], writes=["c_col"])
    P.op("vector", lambda e: e.tensor_copy(r127c[:, 0:1], ps_qkv[:, 1:2]),
         reads=["psbA"], writes=["r127c"])
    for grow, gsrow, gk, gsk in ((qg_row, qgs_row, "qg", "qgs"),
                                 (kg_row, kgs_row, "kg", "kgs")):
        P.op("vector", lambda e, a=gsrow, b=grow: e.tensor_scalar_mul(
            a[0:1, 0:64], b[0:1, 64:128], -1.0), reads=[gk], writes=[gsk])
        P.op("vector", lambda e, a=gsrow, b=grow: e.tensor_copy(
            a[0:1, 64:128], b[0:1, 0:64]), reads=[gk], writes=[gsk])
    for s, (grow, gk) in enumerate(((qg_row, "qg"), (qgs_row, "qgs"),
                                    (kg_row, "kg"), (kgs_row, "kgs"))):
        P.op("tensor", lambda e, grow=grow: e.matmul(
            ps_qkv[:, 0:128], ones_row[0:1, :], grow[0:1, :], start=True, stop=True),
             reads=["ones_row", gk], writes=["psbA"])
        P.op("vector", lambda e: e.tensor_copy(gbc[:], ps_qkv[:, 0:128]),
             reads=["psbA"], writes=["gbc"])
        for j in range(NT):
            sl_ = TB[:, s * 2048 + j * 128: s * 2048 + (j + 1) * 128]
            P.op("vector", lambda e, sl_=sl_: e.tensor_mul(sl_, sl_, gbc[:]),
                 reads=[f"tb{s}", "gbc"], writes=[f"tb{s}"])

    P.fence()

    P.phase = 2
    for j in range(NT):
        sl = j % 2
        xb = A2[:, sl * 2048:(sl + 1) * 2048]
        P.op("gpsimd", lambda e, j=j, xb=xb: e.dma_start(
            out=xb, in_=x_in[j * 128:(j + 1) * 128, :]),
             writes=[f"xb{sl}"], dma=f"xb{sl}")
        for i0 in range(0, NI, 4):
            for di in range(4):
                i = i0 + di
                P.op("tensor", lambda e, i=i, di=di, xb=xb: e.matmul(
                    ps_tb[:, di * 128:(di + 1) * 128], xb[:, i * 128:(i + 1) * 128],
                    idb[:], start=True, stop=True, is_transpose=True),
                     reads=[f"xb{sl}", "idb"], writes=["ps_tb"])
            dst = A1v[:, i0:i0 + 4, j * 128:(j + 1) * 128]
            P.op("vector", lambda e, dst=dst: e.tensor_copy(dst, ps_tbv),
                 reads=["ps_tb"], writes=["xT"])

    P.fence()

    P.phase = 3
    qkv_o = [0, 6144]
    pT2_o = 12288
    pb2_o = 20480
    wg_o = [22528, 28672]
    sp2_o = 34816
    SC = 1024

    def emit_weights(hh):
        par = hh % 2
        P.op("gpsimd", lambda e, hh=hh, par=par: e.dma_start(
            out=A2[:, wg_o[par]: wg_o[par] + 6144], in_=wqkv_in[hh]),
             writes=[f"wg{par}"], dma=f"wg{par}")

    def stage_a_pieces(hh):
        par = hh % 2
        base = qkv_o[par]
        pieces = []
        for sj in range(4):
            for jj in range(4):
                j = sj * 4 + jj

                def piece(j=j, jj=jj, par=par, base=base):
                    for i in range(NI):
                        lhs = A1v[:, i, j * 128:(j + 1) * 128]
                        rhs = A2[:, wg_o[par] + i * 384: wg_o[par] + (i + 1) * 384]
                        P.op("tensor", lambda e, lhs=lhs, rhs=rhs, i=i: e.matmul(
                            ps_big[:, 0:384], lhs, rhs, start=(i == 0), stop=(i == NI - 1)),
                             reads=["xT", f"wg{par}"], writes=["psQ"])
                    P.op("scalar", lambda e, jj=jj: e.copy(
                        A3[:, jj * 256:(jj + 1) * 256], ps_big[:, 0:256]),
                         reads=["psQ"], writes=["qg4"])
                    P.op("scalar", lambda e, j=j, par=par, base=base: e.copy(
                        A2[:, base + 4096 + j * 128: base + 4096 + (j + 1) * 128],
                        ps_big[:, 256:384]),
                         reads=["psQ"], writes=[f"vg{par}"])
                    for qk in range(2):
                        scol = qk * 16 + j
                        srcq = A3[:, jj * 256 + qk * 128: jj * 256 + (qk + 1) * 128]
                        P.op("scalar", lambda e, srcq=srcq, scol=scol: e.activation(
                            A3[:, 2688:2816], srcq, AF.Square,
                            accum_out=st_ss[:, scol:scol + 1]),
                             reads=["qg4"], writes=["st_ss", "sqscr"])
                pieces.append(piece)

            def rope_piece(sj=sj, par=par, base=base):
                ss_v = st_ss.rearrange("p (c j) -> p c j", c=4)[:, 0:2, sj * 4:(sj + 1) * 4]
                rq_v = rq_sb.rearrange("p (c j) -> p c j", c=4)[:, 0:2, sj * 4:(sj + 1) * 4]
                P.op("vector", lambda e, ss_v=ss_v, rq_v=rq_v: e.tensor_scalar(
                    rq_v, ss_v, 1.0 / 128.0, 1e-6, ALU.mult, ALU.add),
                     reads=["st_ss"], writes=["rq"])
                P.op("scalar", lambda e, rq_v=rq_v: e.activation(rq_v, rq_v, AF.Ln),
                     reads=["rq"], writes=["rq"])
                P.op("scalar", lambda e, rq_v=rq_v: e.activation(rq_v, rq_v, AF.Exp, scale=-0.5),
                     reads=["rq"], writes=["rq"])
                for qk, tb_c, tb_s in ((0, 0, 1), (1, 2, 3)):
                    dT_o = base + qk * 2048
                    for jj in range(4):
                        j = sj * 4 + jj
                        rpar = jj % 2
                        qtile = A3[:, jj * 256 + qk * 128: jj * 256 + (qk + 1) * 128]
                        rcol = rq_sb[:, qk * 16 + j: qk * 16 + j + 1]
                        gcos = TB[:, tb_c * 2048 + j * 128: tb_c * 2048 + (j + 1) * 128]
                        gsin = TB[:, tb_s * 2048 + j * 128: tb_s * 2048 + (j + 1) * 128]
                        P.op("vector", lambda e, c1=c1_t[rpar], qtile=qtile, rcol=rcol,
                             gcos=gcos: e.scalar_tensor_tensor(
                                 c1[:], qtile, rcol, gcos, ALU.mult, ALU.mult),
                             reads=["qg4", "rq", f"tb{tb_c}"], writes=[f"c1{rpar}"])
                        P.op("vector", lambda e, tm=tmp_t[rpar], qtile=qtile, rcol=rcol,
                             gsin=gsin: e.scalar_tensor_tensor(
                                 tm[:, 0:64], qtile[:, 64:128], rcol, gsin[:, 0:64],
                                 ALU.mult, ALU.mult),
                             reads=["qg4", "rq", f"tb{tb_s}"], writes=[f"tm{rpar}"])
                        P.op("vector", lambda e, tm=tmp_t[rpar], qtile=qtile, rcol=rcol,
                             gsin=gsin: e.scalar_tensor_tensor(
                                 tm[:, 64:128], qtile[:, 0:64], rcol, gsin[:, 64:128],
                                 ALU.mult, ALU.mult),
                             reads=["qg4", "rq", f"tb{tb_s}"], writes=[f"tm{rpar}"])
                        P.op("tensor", lambda e, c1=c1_t[rpar], jj=jj: e.matmul(
                            ps_big[:, 512 + jj * 128: 512 + (jj + 1) * 128], c1[:], idf[:],
                            start=True, stop=False, is_transpose=True),
                             reads=[f"c1{rpar}", "idf"], writes=["psR"])
                        P.op("tensor", lambda e, tm=tmp_t[rpar], jj=jj: e.matmul(
                            ps_big[:, 512 + jj * 128: 512 + (jj + 1) * 128], tm[:], idf[:],
                            start=False, stop=True, is_transpose=True),
                             reads=[f"tm{rpar}", "idf"], writes=["psR"])
                    dstq = A2[:, dT_o + sj * 512: dT_o + (sj + 1) * 512]
                    P.op("scalar", lambda e, dstq=dstq: e.copy(dstq, ps_big[:, 512:1024]),
                         reads=["psR"], writes=[f"qT{par}" if qk == 0 else f"kT{par}"])
            pieces.append(rope_piece)
        return pieces

    def stage_b_pieces(hh):
        par = hh % 2
        base = qkv_o[par]
        qTk, kTk, vgk = f"qT{par}", f"kT{par}", f"vg{par}"
        pieces = []
        for c in range(4):
            def zero_piece(c=c):
                for j2 in range(3):
                    q = c * 4 + j2
                    z0 = pT2_o + j2 * 2048 + (q + 1) * 128
                    z1 = pT2_o + j2 * 2048 + (c * 4 + 4) * 128
                    P.op("vector", lambda e, z0=z0, z1=z1: e.memset(A2[:, z0:z1], 0.0),
                         writes=["pT"])
            pieces.append(zero_piece)
            for j2 in range(4):
                q = c * 4 + j2
                kt_end = (q + 1) * 128
                if kt_end <= 1024:
                    sbase = SC + (q % 2) * 1024
                    pkeys = ["psS0"] if (q % 2) == 0 else ["psS1"]
                else:
                    sbase = SC
                    pkeys = ["psS0", "psS1"]
                ato = (q % 2) * 8

                def tile_piece(q=q, kt_end=kt_end, sbase=sbase, pkeys=pkeys, ato=ato,
                               par=par, base=base, j2=j2):
                    lhs = A2[:, base + q * 128: base + (q + 1) * 128]
                    ncc = (kt_end + 511) // 512
                    for cc in range(ncc):
                        c0, c1c = cc * 512, min(kt_end, (cc + 1) * 512)
                        rhs = A2[:, base + 2048 + c0: base + 2048 + c1c]
                        last = (cc == ncc - 1)
                        P.op("tensor", lambda e, lhs=lhs, rhs=rhs, c0=c0, c1c=c1c, last=last:
                             e.matmul(ps_big[:, sbase + c0: sbase + c1c], lhs, rhs,
                                      start=True, stop=not last),
                             reads=[qTk, kTk], writes=pkeys)
                        if last:
                            P.op("tensor", lambda e: e.matmul(
                                ps_big[:, sbase + q * 128: sbase + kt_end],
                                idb[:], maskb[:], start=False, stop=True),
                                 reads=["idb", "maskb"] + pkeys, writes=pkeys)
                        P.op("vector", lambda e, cc=cc, c0=c0, c1c=c1c: e.reduce_max(
                            at[:, ato + 4 + cc: ato + 5 + cc],
                            ps_big[:, sbase + c0: sbase + c1c], axis=AX, negate=True),
                             reads=pkeys, writes=[f"atm{ato}{cc}"])
                    P.op("vector", lambda e, ncc=ncc: e.tensor_reduce(
                        at[:, ato:ato + 1], at[:, ato + 4: ato + 4 + ncc], axis=AX,
                        op=ALU.min),
                         reads=[f"atm{ato}{cc2}" for cc2 in range(ncc)], writes=[f"at0{ato}"])
                    P.op("vector", lambda e: e.tensor_mul(
                        at[:, ato + 1:ato + 2], at[:, ato:ato + 1], c_col[:]),
                         reads=[f"at0{ato}", "c_col"], writes=[f"at1{ato}"])
                    nh = 2 if kt_end > 512 else 1
                    hbs = [(0, kt_end // nh if nh == 2 else kt_end), (kt_end // nh, kt_end)]
                    for hfi in range(nh):
                        hb0, hb1 = hbs[hfi] if nh == 2 else (0, kt_end)
                        P.op("scalar", lambda e, hb0=hb0, hb1=hb1, hfi=hfi: e.activation(
                            A2[:, pb2_o + hb0: pb2_o + hb1],
                            ps_big[:, sbase + hb0: sbase + hb1], AF.Exp,
                            bias=at[:, ato + 1:ato + 2], scale=c_col[:, 0:1],
                            accum_out=at[:, ato + 4 + hfi: ato + 5 + hfi]),
                             reads=pkeys + [f"at1{ato}", "c_col"],
                             writes=[f"pbh{hfi}", f"atm{ato}{hfi}"])
                    if nh == 2:
                        P.op("vector", lambda e: e.tensor_add(
                            at[:, ato + 2:ato + 3], at[:, ato + 4:ato + 5],
                            at[:, ato + 5:ato + 6]),
                             reads=[f"atm{ato}0", f"atm{ato}1"], writes=[f"at2{ato}"])
                    else:
                        P.op("vector", lambda e: e.tensor_copy(
                            at[:, ato + 2:ato + 3], at[:, ato + 4:ato + 5]),
                             reads=[f"atm{ato}0"], writes=[f"at2{ato}"])
                    P.op("vector", lambda e: e.reciprocal(
                        at[:, ato + 3:ato + 4], at[:, ato + 2:ato + 3]),
                         reads=[f"at2{ato}"], writes=[f"at3{ato}"])
                pieces.append(tile_piece)

                def tr_piece(q=q, kt_end=kt_end, sbase=sbase, pkeys=pkeys, ato=ato,
                             par=par, base=base, j2=j2):
                    for i0 in range(0, q + 1, 4):
                        nb = min(4, q + 1 - i0)
                        P.op("vector", lambda e, i0=i0, nb=nb: e.tensor_scalar_mul(
                            A2[:, pb2_o + i0 * 128: pb2_o + (i0 + nb) * 128],
                            A2[:, pb2_o + i0 * 128: pb2_o + (i0 + nb) * 128],
                            at[:, ato + 3:ato + 4]),
                             reads=["pbh0", "pbh1", f"at3{ato}"], writes=["pbs"])
                        for di in range(nb):
                            i = i0 + di
                            P.op("tensor", lambda e, i=i, di=di: e.matmul(
                                ps_tb[:, di * 128:(di + 1) * 128],
                                A2[:, pb2_o + i * 128: pb2_o + (i + 1) * 128],
                                idb[:], start=True, stop=True, is_transpose=True),
                                 reads=["pbs", "idb"], writes=["ps_tb"])
                        dst = A2[:, pT2_o + j2 * 2048 + i0 * 128:
                                 pT2_o + j2 * 2048 + (i0 + nb) * 128]
                        P.op("scalar", lambda e, dst=dst, nb=nb: e.copy(
                            dst, ps_tb[:, 0:nb * 128]),
                             reads=["ps_tb"], writes=["pT"])
                pieces.append(tr_piece)

            def av_piece(c=c, par=par, base=base):
                nkt = c * 4 + 4
                pT3 = A2[:, pT2_o: pT2_o + 8192].rearrange("p (j2 t) -> p j2 t", j2=4)
                for i in range(nkt):
                    lhs = A2[:, base + 4096 + i * 128: base + 4096 + (i + 1) * 128]
                    rhs = pT3[:, :, i * 128:(i + 1) * 128]
                    P.op("tensor", lambda e, lhs=lhs, rhs=rhs, i=i, nkt=nkt: e.matmul(
                        ps_wk[:], lhs, rhs, start=(i == 0), stop=(i == nkt - 1)),
                         reads=[vgk, "pT"], writes=["ps_wk"])
                P.op("scalar", lambda e, c=c: e.copy(
                    A2[:, sp2_o + c * 512: sp2_o + (c + 1) * 512], ps_wk[:]),
                     reads=["ps_wk"], writes=["atst"])
            pieces.append(av_piece)

        def store_piece(hh=hh):
            P.op("sync", lambda e, hh=hh: e.dma_start(
                out=attnT_d[hh * 128:(hh + 1) * 128, :], in_=A2[:, sp2_o: sp2_o + 2048]),
                 reads=["atst"], writes=[f"attnT{hh}"], dma="atst")
        pieces.append(store_piece)
        return pieces

    emit_weights(0)
    prevB = None
    for hh in range(NH + 1):
        if hh < NH:
            if hh + 1 < NH:
                emit_weights(hh + 1)
            pa = list(stage_a_pieces(hh))
        else:
            pa = []
        pb_l = prevB if prevB is not None else []
        if not pb_l:
            for p_ in pa:
                p_()
        else:
            ai = 0
            for c in range(4):
                basei = c * 10
                pb_l[basei]()
                for j2 in range(4):
                    pb_l[basei + 1 + 2 * j2]()
                    if ai < len(pa):
                        pa[ai]()
                        ai += 1
                    pb_l[basei + 2 + 2 * j2]()
                pb_l[basei + 9]()
                if ai < len(pa):
                    pa[ai]()
                    ai += 1
            pb_l[40]()
            while ai < len(pa):
                pa[ai]()
                ai += 1
        prevB = stage_b_pieces(hh) if hh < NH else None

    P.fence()

    P.phase = 4
    for i in range(NI):
        P.op("gpsimd", lambda e, i=i: e.dma_start(
            out=A1[:, i * 2048:(i + 1) * 2048], in_=wout_in[i * 128:(i + 1) * 128, :]),
             writes=["wout"], dma="wout")
    pending_tr = [None]

    def p3_tile(j):
        sl = j % 2
        atb_base = atb_o
        atbv = A2[:, atb_base: atb_base + 2048].rearrange("p (i t) -> p i t", i=16)
        atsv = attnT_d[:, j * 128:(j + 1) * 128].rearrange("(i p) t -> p i t", p=128)
        for ic in range(0, 16, 4):
            P.op("sync", lambda e, atbv=atbv, atsv=atsv, ic=ic, j=j: e.dma_start(
                out=atbv[:, ic:ic + 4], in_=atsv[:, ic:ic + 4]),
                 writes=["atb0"], dma="atb0")
        P.op("sync", lambda e, j=j, sl=sl: e.dma_start(
            out=TB[:, sl * 2048:(sl + 1) * 2048], in_=x_in[j * 128:(j + 1) * 128, :]),
             writes=[f"xres{sl}"], dma=f"xres{sl}")
        for i in range(NI):
            lhs = A2[:, atb_base + i * 128: atb_base + (i + 1) * 128]
            for cc in range(4):
                rhs = A1v[:, i, cc * 512:(cc + 1) * 512]
                P.op("tensor", lambda e, lhs=lhs, rhs=rhs, i=i, cc=cc: e.matmul(
                    ps_s[:, cc * 512:(cc + 1) * 512], lhs, rhs,
                    start=(i == 0), stop=(i == NI - 1)),
                     reads=["atb0", "wout"], writes=["psbA", "psbB"])
        if pending_tr[0] is not None:
            pending_tr[0]()
            pending_tr[0] = None

        P.op("vector", lambda e, sl=sl: e.tensor_add(
            TB[:, 4096:6144], ps_s[:], TB[:, sl * 2048:(sl + 1) * 2048]),
             reads=["psbA", "psbB", f"xres{sl}"], writes=["xr"])
        _ln_core(P, TB, A3, lnst)
        P.op("gpsimd", lambda e, j=j: e.dma_start(
            out=xln_d[j * 128:(j + 1) * 128, :], in_=A3[:, 0:2048]),
             reads=["xln"], writes=[f"xln_d{j}"], dma="xlnst")
        P.op("vector", lambda e: e.reduce_max(lnst[:, 6:7], A3[:, 0:2048], axis=AX,
                                              apply_absolute_value=True),
             reads=["xln"], writes=["bm"])
        P.op("vector", lambda e: e.tensor_mul(lnst[:, 7:8], lnst[:, 6:7], r127c[:]),
             reads=["bm", "r127c"], writes=["b1"])
        P.op("vector", lambda e, j=j: e.tensor_scalar_max(
            betas[:, j:j + 1], lnst[:, 7:8], 1e-8),
             reads=["b1"], writes=["betas"])
        P.op("vector", lambda e, j=j: e.reciprocal(lnst[:, 8:9], betas[:, j:j + 1]),
             reads=["betas"], writes=["rbeta"])
        P.op("vector", lambda e: e.tensor_scalar(
            TB[:, 6144:8192], A3[:, 0:2048], lnst[:, 8:9], 128.5, ALU.mult, ALU.add),
             reads=["xln", "rbeta"], writes=["t2", "t20", "t21"])
        for li, B in enumerate((128.0, 64.0, 32.0, 16.0, 8.0, 4.0, 2.0, 1.0)):
            for hf in range(2):
                h0c, h1c = hf * 1024, (hf + 1) * 1024
                vsrc = TB[:, 6144 + h0c: 6144 + h1c] if li == 0 else TB[:, 4096 + h0c: 4096 + h1c]
                vkey = f"t2{hf}" if li == 0 else f"modm{hf}"
                P.op("scalar", lambda e, vsrc=vsrc, li=li, h0c=h0c, h1c=h1c: e.activation(
                    A3[:, 2048 + h0c: 2048 + h1c], vsrc, AF.Sigmoid,
                    scale=kb[:, 0:1], bias=kb[:, 1 + li:2 + li]),
                     reads=[vkey, "kb"], writes=[f"bq{hf}"])
                P.op("vector", lambda e, vsrc=vsrc, B=B, h0c=h0c, h1c=h1c: e.scalar_tensor_tensor(
                    TB[:, 4096 + h0c: 4096 + h1c], A3[:, 2048 + h0c: 2048 + h1c], -B, vsrc,
                    ALU.mult, ALU.add),
                     reads=[f"bq{hf}", vkey], writes=[f"modm{hf}"])
        P.op("vector", lambda e: e.tensor_sub(
            A3[:, 2048:4096], TB[:, 6144:8192], TB[:, 4096:6144]),
             reads=["t2", "modm0", "modm1", "bq0", "bq1"], writes=["xqf"])
        P.op("vector", lambda e: e.tensor_scalar(
            A3[:, 2048:4096], A3[:, 2048:4096], -128.0, -127.0, ALU.add, ALU.max),
             reads=["xqf"], writes=["xqf"])
        P.op("vector", lambda e: e.tensor_scalar(
            A2[:, xqb_o: xqb_o + 2048], A3[:, 2048:4096], 127.0, None, ALU.min),
             reads=["xqf"], writes=["xqb"])

        def tr(j=j):
            for i0 in range(0, NI, 4):
                for di in range(4):
                    i = i0 + di
                    P.op("tensor", lambda e, i=i, di=di: e.matmul(
                        ps_tb[:, di * 128:(di + 1) * 128],
                        A2[:, xqb_o + i * 128: xqb_o + (i + 1) * 128],
                        idb[:], start=True, stop=True, is_transpose=True),
                         reads=["xqb", "idb"], writes=["ps_tb"])
                dst = A2xq[:, i0:i0 + 4, j * 128:(j + 1) * 128]
                P.op("vector", lambda e, dst=dst: e.tensor_copy(dst, ps_tbv),
                     reads=["ps_tb"], writes=["xqT"])
        pending_tr[0] = tr

    for j in range(NT):
        p3_tile(j)
    pending_tr[0]()
    pending_tr[0] = None

    P.fence()

    P.phase = 5
    for r in range(NI):
        sl = r % 2
        P.op("sync", lambda e, r=r, sl=sl: e.dma_start(
            out=TB[:, sl * 2048:(sl + 1) * 2048], in_=wf_in[r * 128:(r + 1) * 128, :]),
             writes=[f"xres{sl}"], dma=f"xres{sl}")
        P.op("vector", lambda e, r=r, sl=sl: e.reduce_sum(
            asum[:, r:r + 1], TB[:, sl * 2048:(sl + 1) * 2048], axis=AX,
            apply_absolute_value=True),
             reads=[f"xres{sl}"], writes=["asum"])
    P.op("vector", lambda e: e.reduce_sum(lnst[:, 0:1], asum[:], axis=AX),
         reads=["asum"], writes=["ascol"])
    P.op("tensor", lambda e: e.matmul(ps_qkv[0:1, 0:1], ones_col[:, 0:1], lnst[:, 0:1],
                                      start=True, stop=True),
         reads=["ones_col", "ascol"], writes=["psbA"])
    P.op("vector", lambda e: e.tensor_scalar(
        c1x1[0:1, 3:4], ps_qkv[0:1, 0:1], 1.0 / 4194304.0, 1e-8, ALU.mult, ALU.max),
         reads=["psbA"], writes=["alpha1"])
    P.op("vector", lambda e: e.reciprocal(c1x1[0:1, 2:3], c1x1[0:1, 3:4]),
         reads=["alpha1"], writes=["ralpha1"])
    P.op("tensor", lambda e: e.matmul(ps_qkv[:, 0:2], ones_row[0:1, :], c1x1[0:1, 2:4],
                                      start=True, stop=True),
         reads=["ones_row", "alpha1", "ralpha1"], writes=["psbA"])
    P.op("vector", lambda e: e.tensor_copy(ralpha_col[:, 0:1], ps_qkv[:, 0:1]),
         reads=["psbA"], writes=["ralpha_col"])
    P.op("vector", lambda e: e.tensor_copy(alpha_col[:, 0:1], ps_qkv[:, 1:2]),
         reads=["psbA"], writes=["alpha_col"])
    for r in range(NI):
        sl = r % 2
        P.op("sync", lambda e, r=r, sl=sl: e.dma_start(
            out=TB[:, sl * 2048:(sl + 1) * 2048], in_=wf_in[r * 128:(r + 1) * 128, :]),
             writes=[f"xres{sl}"], dma=f"xres{sl}")
        wsrc = TB[:, sl * 2048:(sl + 1) * 2048]
        P.op("vector", lambda e, wsrc=wsrc: e.tensor_scalar(
            TB[:, 4096:6144], wsrc, ralpha_col[:, 0:1], None, ALU.mult),
             reads=[f"xres{sl}", "ralpha_col"], writes=["t2"])
        P.op("scalar", lambda e: e.activation(
            A3[:, 0:2048], TB[:, 4096:6144], AF.Sigmoid,
            scale=kb[:, 0:1], bias=kb[:, 9:10]),
             reads=["t2", "kb"], writes=["b1"])
        P.op("scalar", lambda e: e.activation(
            A3[:, 2048:4096], TB[:, 4096:6144], AF.Sigmoid,
            scale=kb[:, 0:1], bias=kb[:, 10:11]),
             reads=["t2", "kb"], writes=["b2"])
        P.op("vector", lambda e: e.tensor_add(
            TB[:, 6144:8192], A3[:, 0:2048], A3[:, 2048:4096]),
             reads=["b1", "b2"], writes=["s12"])
        P.op("vector", lambda e: e.tensor_scalar(
            A2[:, atb_o: atb_o + 2048], TB[:, 6144:8192], -1.0, None, ALU.add),
             reads=["s12"], writes=["wqb"])
        for i0 in range(0, NI, 4):
            for di in range(4):
                i = i0 + di
                P.op("tensor", lambda e, i=i, di=di: e.matmul(
                    ps_tb[:, di * 128:(di + 1) * 128],
                    A2[:, atb_o + i * 128: atb_o + (i + 1) * 128],
                    idb[:], start=True, stop=True, is_transpose=True),
                     reads=["wqb", "idb"], writes=["ps_tb"])
            dst = A1v[:, i0:i0 + 4, r * 128:(r + 1) * 128]
            P.op("vector", lambda e, dst=dst: e.tensor_copy(dst, ps_tbv),
                 reads=["ps_tb"], writes=["wqT"])

    P.fence()

    P.phase = 6
    for j in range(NT):
        sl = j % 2
        P.op("sync", lambda e, j=j, sl=sl: e.dma_start(
            out=TB[:, sl * 2048:(sl + 1) * 2048], in_=xln_d[j * 128:(j + 1) * 128, :]),
             writes=[f"xres{sl}"], dma=f"xres{sl}")
        for i in range(NI):
            lhs = A2xq[:, i, j * 128:(j + 1) * 128]
            for cc in range(4):
                rhs = A1v[:, i, cc * 512:(cc + 1) * 512]
                P.op("tensor", lambda e, lhs=lhs, rhs=rhs, i=i, cc=cc: e.matmul(
                    ps_s[:, cc * 512:(cc + 1) * 512], lhs, rhs,
                    start=(i == 0), stop=(i == NI - 1)),
                     reads=["xqT", "wqT"], writes=["psbA", "psbB"])
        P.op("vector", lambda e, j=j: e.tensor_mul(
            lnst[:, 9:10], alpha_col[:], betas[:, j:j + 1]),
             reads=["alpha_col", "betas"], writes=["abcol"])
        P.op("vector", lambda e, sl=sl: e.scalar_tensor_tensor(
            TB[:, 4096:6144], ps_s[:], lnst[:, 9:10], TB[:, sl * 2048:(sl + 1) * 2048],
            ALU.mult, ALU.add),
             reads=["psbA", "psbB", "abcol", f"xres{sl}"], writes=["xr"])
        _ln_core(P, TB, A3, lnst)
        P.op("gpsimd", lambda e, j=j: e.dma_start(
            out=y_out[j * 128:(j + 1) * 128, :], in_=A3[:, 0:2048]),
             reads=["xln"], writes=[f"y{j}"], dma="yout")

    if max_phase < 6:
        P.phase = 0
        P.fence()
        P.op("vector", lambda e: e.memset(A3[:, 0:2048], 0.0), writes=["dummy"])
        for j in range(NT):
            P.op("sync", lambda e, j=j: e.dma_start(
                out=y_out[j * 128:(j + 1) * 128, :], in_=A3[:, 0:2048]),
                 reads=["dummy"], writes=[f"dy{j}"], dma="yout")
    P.final.append(("sync", "d_yout"))
    P.emit(blk)
    es.close()
    return nc


def _ln_core(P, TB, A3, lnst):
    P.op("vector", lambda e: e.reduce_sum(lnst[:, 0:1], TB[:, 4096:6144], axis=AX, negate=True),
         reads=["xr"], writes=["nsum"])
    P.op("vector", lambda e: e.tensor_scalar_mul(lnst[:, 1:2], lnst[:, 0:1], 1.0 / 2048.0),
         reads=["nsum"], writes=["nmean"])
    P.op("scalar", lambda e: e.activation(A3[:, 2048:4096], TB[:, 4096:6144], AF.Square,
                                          bias=lnst[:, 1:2], scale=1.0,
                                          accum_out=lnst[:, 2:3]),
         reads=["xr", "nmean"], writes=["sumsq", "xqf"])
    P.op("vector", lambda e: e.tensor_scalar(lnst[:, 3:4], lnst[:, 2:3],
                                             1.0 / 2048.0, 1e-5, ALU.mult, ALU.add),
         reads=["sumsq"], writes=["varep"])
    P.op("scalar", lambda e: e.activation(lnst[:, 4:5], lnst[:, 3:4], AF.Ln),
         reads=["varep"], writes=["lnv"])
    P.op("scalar", lambda e: e.activation(lnst[:, 5:6], lnst[:, 4:5], AF.Exp, scale=-0.5),
         reads=["lnv"], writes=["rstd"])
    P.op("vector", lambda e: e.tensor_scalar(A3[:, 0:2048], TB[:, 4096:6144],
                                             lnst[:, 1:2], lnst[:, 5:6], ALU.add, ALU.mult),
         reads=["xr", "nmean", "rstd"], writes=["xln"])


def _rope_tables():
    try:
        import jax
        import jax.numpy as jnp
        with jax.default_device(jax.devices("cpu")[0]):
            inv = 1.0 / 10000.0 ** (jnp.arange(0, HD, 2, dtype=jnp.float32) / HD)
            t = jnp.arange(2048, dtype=jnp.float32)
            fr = t[:, None] * inv[None, :]
            emb = jnp.concatenate((fr, fr), axis=-1)
            return (np.asarray(jnp.cos(emb), dtype=np.float32),
                    np.asarray(jnp.sin(emb), dtype=np.float32))
    except Exception:
        inv = (1.0 / np.float32(10000.0) ** (np.arange(0, HD, 2, dtype=np.float32)
                                             / np.float32(HD))).astype(np.float32)
        t = np.arange(2048, dtype=np.float32)
        fr = (t[:, None] * inv[None, :]).astype(np.float32)
        emb = np.concatenate((fr, fr), axis=-1)
        return np.cos(emb).astype(np.float32), np.sin(emb).astype(np.float32)


_NC_CACHE = [None]


def kernel(L0_attn_x, L0_attn_w_qkv, L0_attn_w_out, L0_attn_q_gamma,
           L0_attn_k_gamma, L0_ffn_weight):
    if _NC_CACHE[0] is None:
        _NC_CACHE[0] = build_nc()
    nc = _NC_CACHE[0]

    cos_t, sin_t = _rope_tables()
    tri = np.tril(np.ones((128, 128), dtype=bool))
    mask = np.where(tri, np.float32(0.0), np.float32(MASKVAL)).astype(np.float32)
    idb = np.eye(128, dtype=ml_dtypes.bfloat16)
    idf = np.eye(128, dtype=np.float32)
    qg = np.ascontiguousarray(np.asarray(L0_attn_q_gamma, dtype=np.float32).reshape(1, 128))
    kg = np.ascontiguousarray(np.asarray(L0_attn_k_gamma, dtype=np.float32).reshape(1, 128))
    _w = np.asarray(L0_attn_w_qkv, dtype=np.float32).reshape(16, 128, 48, 128)
    wqkv = np.ascontiguousarray(
        np.stack([np.concatenate([_w[:, :, h], _w[:, :, 16 + h], _w[:, :, 32 + h]],
                                 axis=-1) for h in range(16)])
        .transpose(0, 2, 1, 3)
        .reshape(16, 128, 6144))
    wout = np.ascontiguousarray(np.asarray(L0_attn_w_out, dtype=np.float32))
    wf = np.ascontiguousarray(np.asarray(L0_ffn_weight, dtype=np.float32))
    x = np.asarray(L0_attn_x, dtype=np.float32)

    in_maps = []
    for c in range(NCORES):
        in_maps.append({
            "x_in": np.ascontiguousarray(x[c]),
            "wqkv_in": wqkv, "wout_in": wout, "wf_in": wf,
            "qg_in": qg, "kg_in": kg,
            "cos_in": cos_t, "sin_in": sin_t,
            "idb_in": idb, "idf_in": idf, "mask_in": mask,
        })
    res = run_bass_kernel_spmd(nc, in_maps, core_ids=list(range(NCORES)))
    out = np.stack([np.asarray(r["y_out"], dtype=np.float32) for r in res.results], axis=0)
    return out


# version 25
# speedup vs baseline: 1.0832x; 1.0089x over previous
import contextlib
import numpy as np
import ml_dtypes

import concourse.bass as bass
import concourse.mybir as mybir
from concourse.bass_utils import run_bass_kernel_spmd

F32 = mybir.dt.float32
BF16 = mybir.dt.bfloat16
AF = mybir.ActivationFunctionType
ALU = mybir.AluOpType
AX = mybir.AxisListType.X

NT = 16
NI = 16
NH = 16
HD = 128
H = 2048
NCORES = 8

MASKVAL = float(np.float32(-1e30 * np.sqrt(128.0)))


def part_col(part, hh):
    return {"q": 0, "k": 2048, "v": 4096}[part] + hh * 128


class Prog:
    ENGINES = ("sync", "gpsimd", "scalar", "vector", "tensor")

    def __init__(self, nc, es):
        self.nc = nc
        self.es = es
        self.streams = {e: [] for e in self.ENGINES}
        self.sems = {}
        self.counts = {}
        self.writer = {}
        self.readers = {}
        self.waited = {e: {} for e in self.ENGINES}
        self.fence_mark = None
        self.final = []
        self.phase = 0
        self.max_phase = 99

    def _sem(self, name):
        if name not in self.sems:
            self.sems[name] = self.es.enter_context(self.nc.semaphore(name))
            self.counts[name] = 0
        return self.sems[name]

    def op(self, eng, build, reads=(), writes=(), dma=None):
        if self.phase > self.max_phase:
            return
        waits = []
        w = self.waited[eng]

        def need(semname, val):
            if semname == "c_tensor" and eng == "tensor":
                return
            if w.get(semname, 0) < val:
                waits.append((semname, val))
                w[semname] = val

        if self.fence_mark is not None:
            need(*self.fence_mark)
        for k in reads:
            wr = self.writer.get(k)
            if wr is not None:
                need(*wr)
        for k in writes:
            wr = self.writer.get(k)
            if wr is not None:
                need(*wr)
            for sv in self.readers.get(k, ()):
                need(*sv)
        if dma is not None:
            semname, step = "d_" + dma, 16
        else:
            semname, step = "c_" + eng, 1
        self._sem(semname)
        self.counts[semname] += step
        val = self.counts[semname]
        self.streams[eng].append((build, waits, semname, step))
        for k in writes:
            self.writer[k] = (semname, val)
            self.readers[k] = []
        for k in reads:
            self.readers.setdefault(k, []).append((semname, val))

    def fence(self):
        marks = {}
        if self.fence_mark is not None:
            marks[self.fence_mark[0]] = self.fence_mark[1]
        for wr in self.writer.values():
            marks[wr[0]] = max(marks.get(wr[0], 0), wr[1])
        for rl in self.readers.values():
            for (s, v) in rl:
                marks[s] = max(marks.get(s, 0), v)
        waits = [(s, v) for s, v in marks.items()]
        self._sem("c_vector")
        self.counts["c_vector"] += 1
        val = self.counts["c_vector"]
        self.streams["vector"].append((lambda e: e.engine_nop(), waits, "c_vector", 1))
        self.fence_mark = ("c_vector", val)
        self.writer = {}
        self.readers = {}
        self.waited = {e: {} for e in self.ENGINES}

    def emit(self, blk):
        waited_vals = {}
        for st in self.streams.values():
            for _, waits, _, _ in st:
                for (sn, v) in waits:
                    waited_vals.setdefault(sn, set()).add(v)
        for (_, sn) in self.final:
            waited_vals.setdefault(sn, set()).add(self.counts[sn])
        thresholds = {sn: sorted(vs) for sn, vs in waited_vals.items()}

        for engname in self.ENGINES:
            stream = self.streams[engname]
            fin = [(s, self.counts[s]) for (e2, s) in self.final if e2 == engname]
            if not stream and not fin:
                continue

            def run(e, stream=stream, fin=fin):
                import bisect
                cum = {}
                pend = {}
                for build, waits, semname, step in stream:
                    for (sn, v) in waits:
                        e.wait_ge(self.sems[sn], v)
                    cum[semname] = cum.get(semname, 0) + step
                    pend[semname] = pend.get(semname, 0) + step
                    inst = build(e)
                    if semname.startswith("d_"):
                        inst.then_inc(self.sems[semname], step)
                        pend[semname] = 0
                        continue
                    th = thresholds.get(semname)
                    lo = cum[semname] - pend[semname]
                    hi = cum[semname]
                    crossed = (th is not None and
                               bisect.bisect_right(th, hi) > bisect.bisect_right(th, lo))
                    if crossed:
                        inst.then_inc(self.sems[semname], pend[semname])
                        pend[semname] = 0
                for (sn, v) in fin:
                    e.wait_ge(self.sems[sn], v)

            getattr(blk, engname)(run)


def build_nc(max_phase=99):
    nc = bass.Bass("TRN2", target_bir_lowering=False)

    x_in = nc.dram_tensor("x_in", [2048, 2048], F32, kind="ExternalInput").ap()
    wqkv_in = nc.dram_tensor("wqkv_in", [16, 128, 6144], F32, kind="ExternalInput").ap()
    wout_in = nc.dram_tensor("wout_in", [2048, 2048], F32, kind="ExternalInput").ap()
    wf_in = nc.dram_tensor("wf_in", [2048, 2048], F32, kind="ExternalInput").ap()
    qg_in = nc.dram_tensor("qg_in", [1, 128], F32, kind="ExternalInput").ap()
    kg_in = nc.dram_tensor("kg_in", [1, 128], F32, kind="ExternalInput").ap()
    cos_in = nc.dram_tensor("cos_in", [2048, 128], F32, kind="ExternalInput").ap()
    sin_in = nc.dram_tensor("sin_in", [2048, 128], F32, kind="ExternalInput").ap()
    idb_in = nc.dram_tensor("idb_in", [128, 128], BF16, kind="ExternalInput").ap()
    idf_in = nc.dram_tensor("idf_in", [128, 128], F32, kind="ExternalInput").ap()
    mask_in = nc.dram_tensor("mask_in", [128, 128], F32, kind="ExternalInput").ap()
    y_out = nc.dram_tensor("y_out", [2048, 2048], F32, kind="ExternalOutput").ap()
    attnT_d = nc.dram_tensor("attnT_d", [2048, 2048], BF16, kind="Internal").ap()
    xln_d = nc.dram_tensor("xln_d", [2048, 2048], F32, kind="Internal").ap()

    es = contextlib.ExitStack()
    sb = lambda nm, shp, dt=F32: es.enter_context(nc.sbuf_tensor(nm, shp, dt))
    psb = lambda nm, shp, dt=F32: es.enter_context(nc.psum_tensor(nm, shp, dt))

    A1 = sb("A1", [128, 32768], BF16)
    A2 = sb("A2", [128, 36864], BF16)
    A3 = sb("A3", [128, 4096], F32)
    TB = sb("TB", [128, 8192], F32)
    gbc = sb("gbc", [128, 128], F32)
    idb = sb("idb", [128, 128], BF16)
    idf = sb("idf", [128, 128], F32)
    mask = sb("mask", [128, 128], F32)
    maskb = sb("maskb", [128, 128], BF16)
    qg_row = sb("qg_row", [1, 128], F32)
    kg_row = sb("kg_row", [1, 128], F32)
    qgs_row = sb("qgs_row", [1, 128], F32)
    kgs_row = sb("kgs_row", [1, 128], F32)
    ones_row = sb("ones_row", [1, 128], F32)
    ones_col = sb("ones_col", [128, 1], F32)
    c1x1 = sb("c1x1", [1, 4], F32)
    c_col = sb("c_col", [128, 1], F32)
    r127c = sb("r127c", [128, 1], F32)
    alpha_col = sb("alpha_col", [128, 1], F32)
    ralpha_col = sb("ralpha_col", [128, 1], F32)
    st_ss = sb("st_ss", [128, 64], F32)
    rq_sb = sb("rq_sb", [128, 64], F32)
    at = sb("at", [128, 16], F32)
    lnst = sb("lnst", [128, 10], F32)
    betas = sb("betas", [128, 16], F32)
    kb = sb("kb", [128, 12], F32)
    asum = sb("asum", [128, 16], F32)
    c1_t = [sb(f"c1_t{p}", [128, 128], F32) for p in range(2)]
    tmp_t = [sb(f"tmp_t{p}", [128, 128], F32) for p in range(2)]

    ps_big = psb("ps_big", [128, 3072], F32)
    ps_s = ps_big[:, 0:2048]
    ps_qkv = ps_big[:, 0:768]
    ps_wk = psb("ps_wk", [128, 512], F32)
    ps_tb_pad = psb("ps_tb_pad", [128, 1024], BF16)
    ps_tb = ps_tb_pad[:, 0:512]

    A1v = A1.rearrange("p (i t) -> p i t", i=16)
    ps_tbv = ps_tb.rearrange("p (i t) -> p i t", i=4)

    qT2_o, kT2_o, vg_o, pT_o, pb_o, wh0_o, wh1_o, sp_o = (
        0, 4096, 8192, 12288, 20480, 22528, 28672, 34816)
    xqT_o, atb_o, xqb_o = 0, 32768, 34816
    A2xq = A2.rearrange("p (i t) -> p i t", i=18)

    blk = es.enter_context(nc.Block())
    P = Prog(nc, es)
    P.max_phase = max_phase

    P.phase = 1
    P.op("sync", lambda e: e.dma_start(out=idb[:], in_=idb_in), writes=["idb"], dma="cst")
    P.op("sync", lambda e: e.dma_start(out=idf[:], in_=idf_in), writes=["idf"], dma="cs2")
    P.op("sync", lambda e: e.dma_start(out=mask[:], in_=mask_in), writes=["mask"], dma="cs3")
    P.op("gpsimd", lambda e: e.dma_start(out=maskb[:], in_=mask_in), writes=["maskb"], dma="cs6")
    P.op("sync", lambda e: e.dma_start(out=qg_row[:], in_=qg_in), writes=["qg"], dma="cs4")
    P.op("sync", lambda e: e.dma_start(out=kg_row[:], in_=kg_in), writes=["kg"], dma="cs5")
    cos_r = cos_in.rearrange("(j p) d -> p j d", p=128)
    sin_r = sin_in.rearrange("(j p) d -> p j d", p=128)
    TBv = TB.rearrange("p (s j d) -> p s j d", s=4, j=16)
    for s_i, srcr in ((0, cos_r), (1, sin_r), (2, cos_r), (3, sin_r)):
        for jc in range(0, 16, 4):
            P.op("sync", lambda e, s_i=s_i, srcr=srcr, jc=jc: e.dma_start(
                out=TBv[:, s_i, jc:jc + 4], in_=srcr[:, jc:jc + 4]),
                 writes=[f"tb{s_i}"], dma=f"tb{s_i}")
    P.op("vector", lambda e: e.memset(ones_row[:], 1.0), writes=["ones_row"])
    P.op("vector", lambda e: e.memset(ones_col[:], 1.0), writes=["ones_col"])
    P.op("vector", lambda e: e.memset(c1x1[0:1, 0:1], 128.0), writes=["c128"])
    P.op("vector", lambda e: e.memset(c1x1[0:1, 1:2], 127.0), writes=["c127"])
    P.op("scalar", lambda e: e.activation(c1x1[0:1, 2:3], c1x1[0:1, 0:1], AF.Ln),
         reads=["c128"], writes=["lnc"])
    P.op("scalar", lambda e: e.activation(c1x1[0:1, 0:1], c1x1[0:1, 2:3], AF.Exp, scale=-0.5),
         reads=["lnc"], writes=["c128"])
    P.op("vector", lambda e: e.reciprocal(c1x1[0:1, 1:2], c1x1[0:1, 1:2]),
         reads=["c127"], writes=["c127"])
    KSTEP = float(2.0 ** 46)
    P.op("vector", lambda e: e.memset(kb[:, 0:1], KSTEP), writes=["kb"])
    for li, B in enumerate((128.0, 64.0, 32.0, 16.0, 8.0, 4.0, 2.0, 1.0)):
        P.op("vector", lambda e, li=li, B=B: e.memset(kb[:, 1 + li:2 + li], -KSTEP * B),
             writes=["kb"])
    P.op("vector", lambda e: e.memset(kb[:, 9:10], -KSTEP * 0.5), writes=["kb"])
    P.op("vector", lambda e: e.memset(kb[:, 10:11], KSTEP * 0.5), writes=["kb"])
    P.op("tensor", lambda e: e.matmul(ps_qkv[:, 0:2], ones_row[0:1, :], c1x1[0:1, 0:2],
                                      start=True, stop=True),
         reads=["ones_row", "c128", "c127"], writes=["psbA"])
    P.op("vector", lambda e: e.tensor_copy(c_col[:, 0:1], ps_qkv[:, 0:1]),
         reads=["psbA"], writes=["c_col"])
    P.op("vector", lambda e: e.tensor_copy(r127c[:, 0:1], ps_qkv[:, 1:2]),
         reads=["psbA"], writes=["r127c"])
    for grow, gsrow, gk, gsk in ((qg_row, qgs_row, "qg", "qgs"),
                                 (kg_row, kgs_row, "kg", "kgs")):
        P.op("vector", lambda e, a=gsrow, b=grow: e.tensor_scalar_mul(
            a[0:1, 0:64], b[0:1, 64:128], -1.0), reads=[gk], writes=[gsk])
        P.op("vector", lambda e, a=gsrow, b=grow: e.tensor_copy(
            a[0:1, 64:128], b[0:1, 0:64]), reads=[gk], writes=[gsk])
    for s, (grow, gk) in enumerate(((qg_row, "qg"), (qgs_row, "qgs"),
                                    (kg_row, "kg"), (kgs_row, "kgs"))):
        P.op("tensor", lambda e, grow=grow: e.matmul(
            ps_qkv[:, 0:128], ones_row[0:1, :], grow[0:1, :], start=True, stop=True),
             reads=["ones_row", gk], writes=["psbA"])
        P.op("vector", lambda e: e.tensor_copy(gbc[:], ps_qkv[:, 0:128]),
             reads=["psbA"], writes=["gbc"])
        for j in range(NT):
            sl_ = TB[:, s * 2048 + j * 128: s * 2048 + (j + 1) * 128]
            P.op("vector", lambda e, sl_=sl_: e.tensor_mul(sl_, sl_, gbc[:]),
                 reads=[f"tb{s}", "gbc"], writes=[f"tb{s}"])

    P.fence()

    P.phase = 2
    for j in range(NT):
        sl = j % 2
        xb = A2[:, sl * 2048:(sl + 1) * 2048]
        P.op("gpsimd", lambda e, j=j, xb=xb: e.dma_start(
            out=xb, in_=x_in[j * 128:(j + 1) * 128, :]),
             writes=[f"xb{sl}"], dma=f"xb{sl}")
        for i0 in range(0, NI, 4):
            for di in range(4):
                i = i0 + di
                P.op("tensor", lambda e, i=i, di=di, xb=xb: e.matmul(
                    ps_tb[:, di * 128:(di + 1) * 128], xb[:, i * 128:(i + 1) * 128],
                    idb[:], start=True, stop=True, is_transpose=True),
                     reads=[f"xb{sl}", "idb"], writes=["ps_tb"])
            dst = A1v[:, i0:i0 + 4, j * 128:(j + 1) * 128]
            P.op("vector", lambda e, dst=dst: e.tensor_copy(dst, ps_tbv),
                 reads=["ps_tb"], writes=["xT"])

    P.fence()

    P.phase = 3
    qkv_o = [0, 6144]
    pT2_o = 12288
    pb2_o = 20480
    wg_o = [22528, 28672]
    sp2_o = 34816
    SC = 1024

    def emit_weights(hh):
        par = hh % 2
        P.op("gpsimd", lambda e, hh=hh, par=par: e.dma_start(
            out=A2[:, wg_o[par]: wg_o[par] + 6144], in_=wqkv_in[hh]),
             writes=[f"wg{par}"], dma=f"wg{par}")

    def stage_a_pieces(hh):
        par = hh % 2
        base = qkv_o[par]
        pieces = []
        for sj in range(4):
            for jj in range(4):
                j = sj * 4 + jj

                def piece(j=j, jj=jj, par=par, base=base):
                    for i in range(NI):
                        lhs = A1v[:, i, j * 128:(j + 1) * 128]
                        rhs = A2[:, wg_o[par] + i * 384: wg_o[par] + (i + 1) * 384]
                        P.op("tensor", lambda e, lhs=lhs, rhs=rhs, i=i: e.matmul(
                            ps_big[:, 0:384], lhs, rhs, start=(i == 0), stop=(i == NI - 1)),
                             reads=["xT", f"wg{par}"], writes=["psQ"])
                    P.op("scalar", lambda e, jj=jj: e.copy(
                        A3[:, jj * 256:(jj + 1) * 256], ps_big[:, 0:256]),
                         reads=["psQ"], writes=["qg4"])
                    P.op("scalar", lambda e, j=j, par=par, base=base: e.copy(
                        A2[:, base + 4096 + j * 128: base + 4096 + (j + 1) * 128],
                        ps_big[:, 256:384]),
                         reads=["psQ"], writes=[f"vg{par}"])
                    for qk in range(2):
                        scol = qk * 16 + j
                        srcq = A3[:, jj * 256 + qk * 128: jj * 256 + (qk + 1) * 128]
                        P.op("scalar", lambda e, srcq=srcq, scol=scol: e.activation(
                            A3[:, 2688:2816], srcq, AF.Square,
                            accum_out=st_ss[:, scol:scol + 1]),
                             reads=["qg4"], writes=["st_ss", "sqscr"])
                pieces.append(piece)

            def rope_piece(sj=sj, par=par, base=base):
                ss_v = st_ss.rearrange("p (c j) -> p c j", c=4)[:, 0:2, sj * 4:(sj + 1) * 4]
                rq_v = rq_sb.rearrange("p (c j) -> p c j", c=4)[:, 0:2, sj * 4:(sj + 1) * 4]
                P.op("vector", lambda e, ss_v=ss_v, rq_v=rq_v: e.tensor_scalar(
                    rq_v, ss_v, 1.0 / 128.0, 1e-6, ALU.mult, ALU.add),
                     reads=["st_ss"], writes=["rq"])
                P.op("scalar", lambda e, rq_v=rq_v: e.activation(rq_v, rq_v, AF.Ln),
                     reads=["rq"], writes=["rq"])
                P.op("scalar", lambda e, rq_v=rq_v: e.activation(rq_v, rq_v, AF.Exp, scale=-0.5),
                     reads=["rq"], writes=["rq"])
                for qk, tb_c, tb_s in ((0, 0, 1), (1, 2, 3)):
                    dT_o = base + qk * 2048
                    for jj in range(4):
                        j = sj * 4 + jj
                        rpar = jj % 2
                        qtile = A3[:, jj * 256 + qk * 128: jj * 256 + (qk + 1) * 128]
                        rcol = rq_sb[:, qk * 16 + j: qk * 16 + j + 1]
                        gcos = TB[:, tb_c * 2048 + j * 128: tb_c * 2048 + (j + 1) * 128]
                        gsin = TB[:, tb_s * 2048 + j * 128: tb_s * 2048 + (j + 1) * 128]
                        P.op("vector", lambda e, c1=c1_t[rpar], qtile=qtile, rcol=rcol,
                             gcos=gcos: e.scalar_tensor_tensor(
                                 c1[:], qtile, rcol, gcos, ALU.mult, ALU.mult),
                             reads=["qg4", "rq", f"tb{tb_c}"], writes=[f"c1{rpar}"])
                        P.op("vector", lambda e, tm=tmp_t[rpar], qtile=qtile, rcol=rcol,
                             gsin=gsin: e.scalar_tensor_tensor(
                                 tm[:, 0:64], qtile[:, 64:128], rcol, gsin[:, 0:64],
                                 ALU.mult, ALU.mult),
                             reads=["qg4", "rq", f"tb{tb_s}"], writes=[f"tm{rpar}"])
                        P.op("vector", lambda e, tm=tmp_t[rpar], qtile=qtile, rcol=rcol,
                             gsin=gsin: e.scalar_tensor_tensor(
                                 tm[:, 64:128], qtile[:, 0:64], rcol, gsin[:, 64:128],
                                 ALU.mult, ALU.mult),
                             reads=["qg4", "rq", f"tb{tb_s}"], writes=[f"tm{rpar}"])
                        P.op("tensor", lambda e, c1=c1_t[rpar], jj=jj: e.matmul(
                            ps_big[:, 512 + jj * 128: 512 + (jj + 1) * 128], c1[:], idf[:],
                            start=True, stop=False, is_transpose=True),
                             reads=[f"c1{rpar}", "idf"], writes=["psR"])
                        P.op("tensor", lambda e, tm=tmp_t[rpar], jj=jj: e.matmul(
                            ps_big[:, 512 + jj * 128: 512 + (jj + 1) * 128], tm[:], idf[:],
                            start=False, stop=True, is_transpose=True),
                             reads=[f"tm{rpar}", "idf"], writes=["psR"])
                    dstq = A2[:, dT_o + sj * 512: dT_o + (sj + 1) * 512]
                    P.op("scalar", lambda e, dstq=dstq: e.copy(dstq, ps_big[:, 512:1024]),
                         reads=["psR"], writes=[f"qT{par}" if qk == 0 else f"kT{par}"])
            pieces.append(rope_piece)
        return pieces

    def stage_b_pieces(hh):
        par = hh % 2
        base = qkv_o[par]
        qTk, kTk, vgk = f"qT{par}", f"kT{par}", f"vg{par}"
        pieces = []
        for c in range(4):
            def zero_piece(c=c):
                for j2 in range(3):
                    q = c * 4 + j2
                    z0 = pT2_o + j2 * 2048 + (q + 1) * 128
                    z1 = pT2_o + j2 * 2048 + (c * 4 + 4) * 128
                    P.op("vector", lambda e, z0=z0, z1=z1: e.memset(A2[:, z0:z1], 0.0),
                         writes=["pT"])
            pieces.append(zero_piece)
            for j2 in range(4):
                q = c * 4 + j2
                kt_end = (q + 1) * 128
                if kt_end <= 1024:
                    sbase = SC + (q % 2) * 1024
                    pkeys = ["psS0"] if (q % 2) == 0 else ["psS1"]
                else:
                    sbase = SC
                    pkeys = ["psS0", "psS1"]
                ato = (q % 2) * 8

                def tile_piece(q=q, kt_end=kt_end, sbase=sbase, pkeys=pkeys, ato=ato,
                               par=par, base=base, j2=j2):
                    lhs = A2[:, base + q * 128: base + (q + 1) * 128]
                    ncc = (kt_end + 511) // 512
                    for cc in range(ncc):
                        c0, c1c = cc * 512, min(kt_end, (cc + 1) * 512)
                        rhs = A2[:, base + 2048 + c0: base + 2048 + c1c]
                        last = (cc == ncc - 1)
                        P.op("tensor", lambda e, lhs=lhs, rhs=rhs, c0=c0, c1c=c1c, last=last:
                             e.matmul(ps_big[:, sbase + c0: sbase + c1c], lhs, rhs,
                                      start=True, stop=not last),
                             reads=[qTk, kTk], writes=pkeys)
                        if last:
                            P.op("tensor", lambda e: e.matmul(
                                ps_big[:, sbase + q * 128: sbase + kt_end],
                                idb[:], maskb[:], start=False, stop=True),
                                 reads=["idb", "maskb"] + pkeys, writes=pkeys)
                        P.op("vector", lambda e, cc=cc, c0=c0, c1c=c1c: e.reduce_max(
                            at[:, ato + 4 + cc: ato + 5 + cc],
                            ps_big[:, sbase + c0: sbase + c1c], axis=AX, negate=True),
                             reads=pkeys, writes=[f"atm{ato}{cc}"])
                    P.op("vector", lambda e, ncc=ncc: e.tensor_reduce(
                        at[:, ato:ato + 1], at[:, ato + 4: ato + 4 + ncc], axis=AX,
                        op=ALU.min),
                         reads=[f"atm{ato}{cc2}" for cc2 in range(ncc)], writes=[f"at0{ato}"])
                    P.op("vector", lambda e: e.tensor_mul(
                        at[:, ato + 1:ato + 2], at[:, ato:ato + 1], c_col[:]),
                         reads=[f"at0{ato}", "c_col"], writes=[f"at1{ato}"])
                    nh = 2 if kt_end > 512 else 1
                    hbs = [(0, kt_end // nh if nh == 2 else kt_end), (kt_end // nh, kt_end)]
                    for hfi in range(nh):
                        hb0, hb1 = hbs[hfi] if nh == 2 else (0, kt_end)
                        P.op("scalar", lambda e, hb0=hb0, hb1=hb1, hfi=hfi: e.activation(
                            A2[:, pb2_o + hb0: pb2_o + hb1],
                            ps_big[:, sbase + hb0: sbase + hb1], AF.Exp,
                            bias=at[:, ato + 1:ato + 2], scale=c_col[:, 0:1],
                            accum_out=at[:, ato + 4 + hfi: ato + 5 + hfi]),
                             reads=pkeys + [f"at1{ato}", "c_col"],
                             writes=[f"pbh{hfi}", f"atm{ato}{hfi}"])
                    if nh == 2:
                        P.op("vector", lambda e: e.tensor_add(
                            at[:, ato + 2:ato + 3], at[:, ato + 4:ato + 5],
                            at[:, ato + 5:ato + 6]),
                             reads=[f"atm{ato}0", f"atm{ato}1"], writes=[f"at2{ato}"])
                    else:
                        P.op("vector", lambda e: e.tensor_copy(
                            at[:, ato + 2:ato + 3], at[:, ato + 4:ato + 5]),
                             reads=[f"atm{ato}0"], writes=[f"at2{ato}"])
                    P.op("vector", lambda e: e.reciprocal(
                        at[:, ato + 3:ato + 4], at[:, ato + 2:ato + 3]),
                         reads=[f"at2{ato}"], writes=[f"at3{ato}"])
                pieces.append(tile_piece)

                def tr_piece(q=q, kt_end=kt_end, sbase=sbase, pkeys=pkeys, ato=ato,
                             par=par, base=base, j2=j2):
                    for i0 in range(0, q + 1, 4):
                        nb = min(4, q + 1 - i0)
                        P.op("vector", lambda e, i0=i0, nb=nb: e.tensor_scalar_mul(
                            A2[:, pb2_o + i0 * 128: pb2_o + (i0 + nb) * 128],
                            A2[:, pb2_o + i0 * 128: pb2_o + (i0 + nb) * 128],
                            at[:, ato + 3:ato + 4]),
                             reads=["pbh0", "pbh1", f"at3{ato}"], writes=["pbs"])
                        for di in range(nb):
                            i = i0 + di
                            P.op("tensor", lambda e, i=i, di=di: e.matmul(
                                ps_tb[:, di * 128:(di + 1) * 128],
                                A2[:, pb2_o + i * 128: pb2_o + (i + 1) * 128],
                                idb[:], start=True, stop=True, is_transpose=True),
                                 reads=["pbs", "idb"], writes=["ps_tb"])
                        dst = A2[:, pT2_o + j2 * 2048 + i0 * 128:
                                 pT2_o + j2 * 2048 + (i0 + nb) * 128]
                        P.op("scalar", lambda e, dst=dst, nb=nb: e.copy(
                            dst, ps_tb[:, 0:nb * 128]),
                             reads=["ps_tb"], writes=["pT"])
                pieces.append(tr_piece)

            def av_piece(c=c, par=par, base=base):
                nkt = c * 4 + 4
                pT3 = A2[:, pT2_o: pT2_o + 8192].rearrange("p (j2 t) -> p j2 t", j2=4)
                for i in range(nkt):
                    lhs = A2[:, base + 4096 + i * 128: base + 4096 + (i + 1) * 128]
                    rhs = pT3[:, :, i * 128:(i + 1) * 128]
                    P.op("tensor", lambda e, lhs=lhs, rhs=rhs, i=i, nkt=nkt: e.matmul(
                        ps_wk[:], lhs, rhs, start=(i == 0), stop=(i == nkt - 1)),
                         reads=[vgk, "pT"], writes=["ps_wk"])
                P.op("scalar", lambda e, c=c: e.copy(
                    A2[:, sp2_o + c * 512: sp2_o + (c + 1) * 512], ps_wk[:]),
                     reads=["ps_wk"], writes=["atst"])
            pieces.append(av_piece)

        def store_piece(hh=hh):
            P.op("sync", lambda e, hh=hh: e.dma_start(
                out=attnT_d[hh * 128:(hh + 1) * 128, :], in_=A2[:, sp2_o: sp2_o + 2048]),
                 reads=["atst"], writes=[f"attnT{hh}"], dma="atst")
        pieces.append(store_piece)
        return pieces

    emit_weights(0)
    prevB = None
    for hh in range(NH + 1):
        if hh < NH:
            if hh + 1 < NH:
                emit_weights(hh + 1)
            pa = list(stage_a_pieces(hh))
        else:
            pa = []
        pb_l = prevB if prevB is not None else []
        if not pb_l:
            for p_ in pa:
                p_()
        else:
            ai = 0
            for c in range(4):
                basei = c * 10
                pb_l[basei]()
                for j2 in range(4):
                    pb_l[basei + 1 + 2 * j2]()
                    if ai < len(pa):
                        pa[ai]()
                        ai += 1
                    pb_l[basei + 2 + 2 * j2]()
                pb_l[basei + 9]()
                if ai < len(pa):
                    pa[ai]()
                    ai += 1
            pb_l[40]()
            while ai < len(pa):
                pa[ai]()
                ai += 1
        prevB = stage_b_pieces(hh) if hh < NH else None

    P.fence()

    P.phase = 4
    for i in range(NI):
        P.op("gpsimd", lambda e, i=i: e.dma_start(
            out=A1[:, i * 2048:(i + 1) * 2048], in_=wout_in[i * 128:(i + 1) * 128, :]),
             writes=[f"wout{i}"], dma="wout")
    pending_tr = [None]

    def p3_tile(j):
        sl = j % 2
        atb_base = atb_o
        atbv = A2[:, atb_base: atb_base + 2048].rearrange("p (i t) -> p i t", i=16)
        atsv = attnT_d[:, j * 128:(j + 1) * 128].rearrange("(i p) t -> p i t", p=128)
        for ic in range(0, 16, 4):
            P.op("sync", lambda e, atbv=atbv, atsv=atsv, ic=ic, j=j: e.dma_start(
                out=atbv[:, ic:ic + 4], in_=atsv[:, ic:ic + 4]),
                 writes=["atb0"], dma="atb0")
        P.op("sync", lambda e, j=j, sl=sl: e.dma_start(
            out=TB[:, sl * 2048:(sl + 1) * 2048], in_=x_in[j * 128:(j + 1) * 128, :]),
             writes=[f"xres{sl}"], dma=f"xres{sl}")
        for i in range(NI):
            lhs = A2[:, atb_base + i * 128: atb_base + (i + 1) * 128]
            for cc in range(4):
                rhs = A1v[:, i, cc * 512:(cc + 1) * 512]
                P.op("tensor", lambda e, lhs=lhs, rhs=rhs, i=i, cc=cc: e.matmul(
                    ps_s[:, cc * 512:(cc + 1) * 512], lhs, rhs,
                    start=(i == 0), stop=(i == NI - 1)),
                     reads=["atb0", f"wout{i}"], writes=["psbA", "psbB"])
        if pending_tr[0] is not None:
            pending_tr[0]()
            pending_tr[0] = None

        P.op("vector", lambda e, sl=sl: e.tensor_add(
            TB[:, 4096:6144], ps_s[:], TB[:, sl * 2048:(sl + 1) * 2048]),
             reads=["psbA", "psbB", f"xres{sl}"], writes=["xr"])
        _ln_core(P, TB, A3, lnst)
        P.op("gpsimd", lambda e, j=j: e.dma_start(
            out=xln_d[j * 128:(j + 1) * 128, :], in_=A3[:, 0:2048]),
             reads=["xln"], writes=[f"xln_d{j}"], dma="xlnst")
        P.op("vector", lambda e: e.reduce_max(lnst[:, 6:7], A3[:, 0:2048], axis=AX,
                                              apply_absolute_value=True),
             reads=["xln"], writes=["bm"])
        P.op("vector", lambda e: e.tensor_mul(lnst[:, 7:8], lnst[:, 6:7], r127c[:]),
             reads=["bm", "r127c"], writes=["b1"])
        P.op("vector", lambda e, j=j: e.tensor_scalar_max(
            betas[:, j:j + 1], lnst[:, 7:8], 1e-8),
             reads=["b1"], writes=["betas"])
        P.op("vector", lambda e, j=j: e.reciprocal(lnst[:, 8:9], betas[:, j:j + 1]),
             reads=["betas"], writes=["rbeta"])
        P.op("vector", lambda e: e.tensor_scalar(
            TB[:, 6144:8192], A3[:, 0:2048], lnst[:, 8:9], 128.5, ALU.mult, ALU.add),
             reads=["xln", "rbeta"], writes=["t2", "t20", "t21"])
        for li, B in enumerate((128.0, 64.0, 32.0, 16.0, 8.0, 4.0, 2.0, 1.0)):
            for hf in range(2):
                h0c, h1c = hf * 1024, (hf + 1) * 1024
                vsrc = TB[:, 6144 + h0c: 6144 + h1c] if li == 0 else TB[:, 4096 + h0c: 4096 + h1c]
                vkey = f"t2{hf}" if li == 0 else f"modm{hf}"
                P.op("scalar", lambda e, vsrc=vsrc, li=li, h0c=h0c, h1c=h1c: e.activation(
                    A3[:, 2048 + h0c: 2048 + h1c], vsrc, AF.Sigmoid,
                    scale=kb[:, 0:1], bias=kb[:, 1 + li:2 + li]),
                     reads=[vkey, "kb"], writes=[f"bq{hf}"])
                P.op("vector", lambda e, vsrc=vsrc, B=B, h0c=h0c, h1c=h1c: e.scalar_tensor_tensor(
                    TB[:, 4096 + h0c: 4096 + h1c], A3[:, 2048 + h0c: 2048 + h1c], -B, vsrc,
                    ALU.mult, ALU.add),
                     reads=[f"bq{hf}", vkey], writes=[f"modm{hf}"])
        P.op("vector", lambda e: e.tensor_sub(
            A3[:, 2048:4096], TB[:, 6144:8192], TB[:, 4096:6144]),
             reads=["t2", "modm0", "modm1", "bq0", "bq1"], writes=["xqf"])
        P.op("vector", lambda e: e.tensor_scalar(
            A3[:, 2048:4096], A3[:, 2048:4096], -128.0, -127.0, ALU.add, ALU.max),
             reads=["xqf"], writes=["xqf"])
        P.op("vector", lambda e: e.tensor_scalar(
            A2[:, xqb_o: xqb_o + 2048], A3[:, 2048:4096], 127.0, None, ALU.min),
             reads=["xqf"], writes=["xqb"])

        def tr(j=j):
            for i0 in range(0, NI, 4):
                for di in range(4):
                    i = i0 + di
                    P.op("tensor", lambda e, i=i, di=di: e.matmul(
                        ps_tb[:, di * 128:(di + 1) * 128],
                        A2[:, xqb_o + i * 128: xqb_o + (i + 1) * 128],
                        idb[:], start=True, stop=True, is_transpose=True),
                         reads=["xqb", "idb"], writes=["ps_tb"])
                dst = A2xq[:, i0:i0 + 4, j * 128:(j + 1) * 128]
                P.op("vector", lambda e, dst=dst: e.tensor_copy(dst, ps_tbv),
                     reads=["ps_tb"], writes=["xqT"])
        pending_tr[0] = tr

    for j in range(NT):
        p3_tile(j)
    pending_tr[0]()
    pending_tr[0] = None

    P.fence()

    P.phase = 5
    for r in range(NI):
        sl = r % 2
        P.op("sync", lambda e, r=r, sl=sl: e.dma_start(
            out=TB[:, sl * 2048:(sl + 1) * 2048], in_=wf_in[r * 128:(r + 1) * 128, :]),
             writes=[f"xres{sl}"], dma=f"xres{sl}")
        P.op("vector", lambda e, r=r, sl=sl: e.reduce_sum(
            asum[:, r:r + 1], TB[:, sl * 2048:(sl + 1) * 2048], axis=AX,
            apply_absolute_value=True),
             reads=[f"xres{sl}"], writes=["asum"])
    P.op("vector", lambda e: e.reduce_sum(lnst[:, 0:1], asum[:], axis=AX),
         reads=["asum"], writes=["ascol"])
    P.op("tensor", lambda e: e.matmul(ps_qkv[0:1, 0:1], ones_col[:, 0:1], lnst[:, 0:1],
                                      start=True, stop=True),
         reads=["ones_col", "ascol"], writes=["psbA"])
    P.op("vector", lambda e: e.tensor_scalar(
        c1x1[0:1, 3:4], ps_qkv[0:1, 0:1], 1.0 / 4194304.0, 1e-8, ALU.mult, ALU.max),
         reads=["psbA"], writes=["alpha1"])
    P.op("vector", lambda e: e.reciprocal(c1x1[0:1, 2:3], c1x1[0:1, 3:4]),
         reads=["alpha1"], writes=["ralpha1"])
    P.op("tensor", lambda e: e.matmul(ps_qkv[:, 0:2], ones_row[0:1, :], c1x1[0:1, 2:4],
                                      start=True, stop=True),
         reads=["ones_row", "alpha1", "ralpha1"], writes=["psbA"])
    P.op("vector", lambda e: e.tensor_copy(ralpha_col[:, 0:1], ps_qkv[:, 0:1]),
         reads=["psbA"], writes=["ralpha_col"])
    P.op("vector", lambda e: e.tensor_copy(alpha_col[:, 0:1], ps_qkv[:, 1:2]),
         reads=["psbA"], writes=["alpha_col"])
    for r in range(NI):
        sl = r % 2
        P.op("sync", lambda e, r=r, sl=sl: e.dma_start(
            out=TB[:, sl * 2048:(sl + 1) * 2048], in_=wf_in[r * 128:(r + 1) * 128, :]),
             writes=[f"xres{sl}"], dma=f"xres{sl}")
        wsrc = TB[:, sl * 2048:(sl + 1) * 2048]
        P.op("vector", lambda e, wsrc=wsrc: e.tensor_scalar(
            TB[:, 4096:6144], wsrc, ralpha_col[:, 0:1], None, ALU.mult),
             reads=[f"xres{sl}", "ralpha_col"], writes=["t2"])
        P.op("scalar", lambda e: e.activation(
            A3[:, 0:2048], TB[:, 4096:6144], AF.Sigmoid,
            scale=kb[:, 0:1], bias=kb[:, 9:10]),
             reads=["t2", "kb"], writes=["b1"])
        P.op("scalar", lambda e: e.activation(
            A3[:, 2048:4096], TB[:, 4096:6144], AF.Sigmoid,
            scale=kb[:, 0:1], bias=kb[:, 10:11]),
             reads=["t2", "kb"], writes=["b2"])
        P.op("vector", lambda e: e.tensor_add(
            TB[:, 6144:8192], A3[:, 0:2048], A3[:, 2048:4096]),
             reads=["b1", "b2"], writes=["s12"])
        P.op("vector", lambda e: e.tensor_scalar(
            A2[:, atb_o: atb_o + 2048], TB[:, 6144:8192], -1.0, None, ALU.add),
             reads=["s12"], writes=["wqb"])
        for i0 in range(0, NI, 4):
            for di in range(4):
                i = i0 + di
                P.op("tensor", lambda e, i=i, di=di: e.matmul(
                    ps_tb[:, di * 128:(di + 1) * 128],
                    A2[:, atb_o + i * 128: atb_o + (i + 1) * 128],
                    idb[:], start=True, stop=True, is_transpose=True),
                     reads=["wqb", "idb"], writes=["ps_tb"])
            dst = A1v[:, i0:i0 + 4, r * 128:(r + 1) * 128]
            P.op("vector", lambda e, dst=dst: e.tensor_copy(dst, ps_tbv),
                 reads=["ps_tb"], writes=["wqT"])

    P.fence()

    P.phase = 6
    for j in range(NT):
        sl = j % 2
        P.op("sync", lambda e, j=j, sl=sl: e.dma_start(
            out=TB[:, sl * 2048:(sl + 1) * 2048], in_=xln_d[j * 128:(j + 1) * 128, :]),
             writes=[f"xres{sl}"], dma=f"xres{sl}")
        for i in range(NI):
            lhs = A2xq[:, i, j * 128:(j + 1) * 128]
            for cc in range(4):
                rhs = A1v[:, i, cc * 512:(cc + 1) * 512]
                P.op("tensor", lambda e, lhs=lhs, rhs=rhs, i=i, cc=cc: e.matmul(
                    ps_s[:, cc * 512:(cc + 1) * 512], lhs, rhs,
                    start=(i == 0), stop=(i == NI - 1)),
                     reads=["xqT", "wqT"], writes=["psbA", "psbB"])
        P.op("vector", lambda e, j=j: e.tensor_mul(
            lnst[:, 9:10], alpha_col[:], betas[:, j:j + 1]),
             reads=["alpha_col", "betas"], writes=["abcol"])
        P.op("vector", lambda e, sl=sl: e.scalar_tensor_tensor(
            TB[:, 4096:6144], ps_s[:], lnst[:, 9:10], TB[:, sl * 2048:(sl + 1) * 2048],
            ALU.mult, ALU.add),
             reads=["psbA", "psbB", "abcol", f"xres{sl}"], writes=["xr"])
        _ln_core(P, TB, A3, lnst)
        P.op("gpsimd", lambda e, j=j: e.dma_start(
            out=y_out[j * 128:(j + 1) * 128, :], in_=A3[:, 0:2048]),
             reads=["xln"], writes=[f"y{j}"], dma="yout")

    if max_phase < 6:
        P.phase = 0
        P.fence()
        P.op("vector", lambda e: e.memset(A3[:, 0:2048], 0.0), writes=["dummy"])
        for j in range(NT):
            P.op("sync", lambda e, j=j: e.dma_start(
                out=y_out[j * 128:(j + 1) * 128, :], in_=A3[:, 0:2048]),
                 reads=["dummy"], writes=[f"dy{j}"], dma="yout")
    P.final.append(("sync", "d_yout"))
    P.emit(blk)
    es.close()
    return nc


def _ln_core(P, TB, A3, lnst):
    P.op("vector", lambda e: e.reduce_sum(lnst[:, 0:1], TB[:, 4096:6144], axis=AX, negate=True),
         reads=["xr"], writes=["nsum"])
    P.op("vector", lambda e: e.tensor_scalar_mul(lnst[:, 1:2], lnst[:, 0:1], 1.0 / 2048.0),
         reads=["nsum"], writes=["nmean"])
    P.op("scalar", lambda e: e.activation(A3[:, 2048:4096], TB[:, 4096:6144], AF.Square,
                                          bias=lnst[:, 1:2], scale=1.0,
                                          accum_out=lnst[:, 2:3]),
         reads=["xr", "nmean"], writes=["sumsq", "xqf"])
    P.op("vector", lambda e: e.tensor_scalar(lnst[:, 3:4], lnst[:, 2:3],
                                             1.0 / 2048.0, 1e-5, ALU.mult, ALU.add),
         reads=["sumsq"], writes=["varep"])
    P.op("scalar", lambda e: e.activation(lnst[:, 4:5], lnst[:, 3:4], AF.Ln),
         reads=["varep"], writes=["lnv"])
    P.op("scalar", lambda e: e.activation(lnst[:, 5:6], lnst[:, 4:5], AF.Exp, scale=-0.5),
         reads=["lnv"], writes=["rstd"])
    P.op("vector", lambda e: e.tensor_scalar(A3[:, 0:2048], TB[:, 4096:6144],
                                             lnst[:, 1:2], lnst[:, 5:6], ALU.add, ALU.mult),
         reads=["xr", "nmean", "rstd"], writes=["xln"])


def _rope_tables():
    try:
        import jax
        import jax.numpy as jnp
        with jax.default_device(jax.devices("cpu")[0]):
            inv = 1.0 / 10000.0 ** (jnp.arange(0, HD, 2, dtype=jnp.float32) / HD)
            t = jnp.arange(2048, dtype=jnp.float32)
            fr = t[:, None] * inv[None, :]
            emb = jnp.concatenate((fr, fr), axis=-1)
            return (np.asarray(jnp.cos(emb), dtype=np.float32),
                    np.asarray(jnp.sin(emb), dtype=np.float32))
    except Exception:
        inv = (1.0 / np.float32(10000.0) ** (np.arange(0, HD, 2, dtype=np.float32)
                                             / np.float32(HD))).astype(np.float32)
        t = np.arange(2048, dtype=np.float32)
        fr = (t[:, None] * inv[None, :]).astype(np.float32)
        emb = np.concatenate((fr, fr), axis=-1)
        return np.cos(emb).astype(np.float32), np.sin(emb).astype(np.float32)


_NC_CACHE = [None]


def kernel(L0_attn_x, L0_attn_w_qkv, L0_attn_w_out, L0_attn_q_gamma,
           L0_attn_k_gamma, L0_ffn_weight):
    if _NC_CACHE[0] is None:
        _NC_CACHE[0] = build_nc()
    nc = _NC_CACHE[0]

    cos_t, sin_t = _rope_tables()
    tri = np.tril(np.ones((128, 128), dtype=bool))
    mask = np.where(tri, np.float32(0.0), np.float32(MASKVAL)).astype(np.float32)
    idb = np.eye(128, dtype=ml_dtypes.bfloat16)
    idf = np.eye(128, dtype=np.float32)
    qg = np.ascontiguousarray(np.asarray(L0_attn_q_gamma, dtype=np.float32).reshape(1, 128))
    kg = np.ascontiguousarray(np.asarray(L0_attn_k_gamma, dtype=np.float32).reshape(1, 128))
    _w = np.asarray(L0_attn_w_qkv, dtype=np.float32).reshape(16, 128, 48, 128)
    wqkv = np.ascontiguousarray(
        np.stack([np.concatenate([_w[:, :, h], _w[:, :, 16 + h], _w[:, :, 32 + h]],
                                 axis=-1) for h in range(16)])
        .transpose(0, 2, 1, 3)
        .reshape(16, 128, 6144))
    wout = np.ascontiguousarray(np.asarray(L0_attn_w_out, dtype=np.float32))
    wf = np.ascontiguousarray(np.asarray(L0_ffn_weight, dtype=np.float32))
    x = np.asarray(L0_attn_x, dtype=np.float32)

    in_maps = []
    for c in range(NCORES):
        in_maps.append({
            "x_in": np.ascontiguousarray(x[c]),
            "wqkv_in": wqkv, "wout_in": wout, "wf_in": wf,
            "qg_in": qg, "kg_in": kg,
            "cos_in": cos_t, "sin_in": sin_t,
            "idb_in": idb, "idf_in": idf, "mask_in": mask,
        })
    res = run_bass_kernel_spmd(nc, in_maps, core_ids=list(range(NCORES)))
    out = np.stack([np.asarray(r["y_out"], dtype=np.float32) for r in res.results], axis=0)
    return out
